# version 08
# baseline (speedup 1.0000x reference)
import numpy as np
from contextlib import ExitStack

import concourse.bass as bass
import concourse.mybir as mybir
from concourse.bass_utils import run_bass_kernel_spmd

F32 = mybir.dt.float32
F32R = mybir.dt.float32r
BF16 = mybir.dt.bfloat16
AF = mybir.ActivationFunctionType
ALU = mybir.AluOpType
AX = mybir.AxisListType

B_SZ = 2
SEQ = 8192
WCAST_A = (2 * 8 * 4096 * 1024) // 8
WCAST_S = 35127296 // 8 * 0 + (2 * 8 * 4096 * 1024 + 1024 * 1024) // 8
D_MODEL = 1024
D_INNER = 2048
D_STATE = 16
N_CORES = 8
DSL = D_INNER // 4
N_HEADS = 16
HEAD_DIM = 64
D_LATENT = 256
N_EXP = 8
D_FF = 4096
EPS = 1e-5
NEG = -30000.0

ENGINES = ("sync", "tensor", "vector", "scalar", "gpsimd")


def f32(x):
    return np.ascontiguousarray(x, dtype=np.float32)



class Emitter:

    SERIAL = False

    def __init__(self, name="prog"):
        self.nc = bass.Bass("TRN2", target_bir_lowering=False)
        self.es = ExitStack()
        self.ops = []
        self.last_w = {}
        self.readers = {}
        self.op_tok = []
        self.eng_cnt = {e: 0 for e in ENGINES}
        self.dma_cnt = {}
        self._sems = {}
        self._final_markers = []

    def dram(self, name, shape, dtype, kind="ExternalInput"):
        return self.nc.dram_tensor(name, shape, dtype, kind=kind).ap()

    def sbuf(self, name, shape, dtype):
        return self.es.enter_context(self.nc.sbuf_tensor(name, shape, dtype))

    def psum(self, name, shape, dtype=F32):
        return self.es.enter_context(self.nc.psum_tensor(name, shape, dtype))

    def op(self, engine, fn, reads=(), writes=(), dma_stream=None):
        opid = len(self.ops)
        deps = set()
        if Emitter.SERIAL and opid > 0:
            deps.add(opid - 1)
        for r in reads:
            if r in self.last_w:
                deps.add(self.last_w[r])
        for w in writes:
            if w in self.last_w:
                deps.add(self.last_w[w])
            for rd in self.readers.get(w, ()):
                deps.add(rd)
        for r in reads:
            self.readers.setdefault(r, []).append(opid)
        for w in writes:
            self.last_w[w] = opid
            self.readers[w] = []
        if dma_stream is not None:
            c = self.dma_cnt.get(dma_stream, 0) + 16
            self.dma_cnt[dma_stream] = c
            tok = (("dma", dma_stream), c)
        else:
            c = self.eng_cnt[engine] + 1
            self.eng_cnt[engine] = c
            tok = (("eng", engine), c)
        self.op_tok.append(tok)
        self.ops.append((engine, fn, sorted(deps), opid, dma_stream))
        return opid

    def dma(self, out, in_, reads=(), writes=(), queue="sync", stream=None):
        if stream is None:
            stream = str(writes[0]) if writes else "out"

        def fn(eng):
            return eng.dma_start(out=out, in_=in_)
        return self.op(queue, fn, reads=reads, writes=writes, dma_stream=stream)

    def final_wait(self, engine="gpsimd"):
        self._final_markers.append(engine)

    def realize(self):
        nc = self.nc
        sems = {}
        for e in ENGINES:
            if self.eng_cnt[e]:
                sems[("eng", e)] = self.es.enter_context(nc.semaphore(f"s_{e}"))
        for s in self.dma_cnt:
            sems[("dma", s)] = self.es.enter_context(nc.semaphore(f"d_{s}"))
        self._sems = sems
        per_eng = {e: [] for e in ENGINES}
        for (engine, fn, deps, opid, dstream) in self.ops:
            per_eng[engine].append((fn, deps, opid, dstream))
        blk = self.es.enter_context(nc.Block())
        with self.es:
            for e in ENGINES:
                lst = per_eng[e]
                if not lst and e not in self._final_markers:
                    continue

                def make(e=e, lst=lst):
                    def body(eng):
                        clock = {}
                        own = 0
                        for fn, deps, opid, dstream in lst:
                            need = {}
                            for d in deps:
                                key, val = self.op_tok[d]
                                if key == ("eng", e):
                                    if e != "tensor" and dstream is None:
                                        if need.get(key, 0) < val:
                                            need[key] = val
                                    continue
                                if key[0] == "dma":
                                    pass
                                if need.get(key, 0) < val:
                                    need[key] = val
                            for key, val in sorted(need.items(), key=lambda kv: str(kv[0])):
                                if clock.get(key, 0) >= val:
                                    continue
                                eng.wait_ge(sems[key], val)
                                clock[key] = val
                            ins = fn(eng)
                            key, val = self.op_tok[opid]
                            ins.then_inc(sems[key], 16 if key[0] == "dma" else 1)
                            if key == ("eng", e):
                                own = val
                        if e in self._final_markers:
                            for s, c in sorted(self.dma_cnt.items()):
                                if ("dma", s) in sems:
                                    eng.wait_ge(sems[("dma", s)], c)
                    return body

                getattr(blk, e)(make())
        return nc


def emit_matmul_acc(em, psum_ap, pairs, reads, writes, extra_dep_resources=()):
    def fn(eng):
        ins = None
        n = len(pairs)
        for i, (l, r) in enumerate(pairs):
            ins = eng.matmul(out=psum_ap, lhsT=l, rhs=r,
                             start=(i == 0), stop=(i == n - 1))
        return ins
    return em.op("tensor", fn, reads=tuple(reads) + tuple(extra_dep_resources),
                 writes=writes)



def build_A():
    em = Emitter("A")
    xT = em.dram("xT", [D_MODEL, SEQ + 3], F32R)
    ipT = em.dram("ipT", [D_MODEL, 1024], F32R)
    convw = em.dram("convw", [128, 16], F32)
    convb = em.dram("convb", [128, 4], F32)
    dcol = em.dram("dcol", [128, 4], F32)
    xpT = em.dram("xpT", [128, 4 * 33], F32R)
    xc_o = em.dram("xc", [DSL, SEQ], F32, kind="ExternalOutput")
    g1_o = em.dram("g1", [DSL, SEQ], F32, kind="ExternalOutput")
    g2_o = em.dram("g2", [DSL, SEQ], F32, kind="ExternalOutput")
    xpp_o = em.dram("xpp", [33, SEQ], F32, kind="ExternalOutput")
    wsrc = em.dram("wsrc", [WCAST_A // 4096, 4096], F32)
    wbf_o = em.dram("wbf", [WCAST_A // 4096, 4096], BF16, kind="ExternalOutput")

    TB = 1024
    NB = SEQ // TB
    H = 3
    TBH = TB + H

    xt_b = em.sbuf("xt_b", [128, 8 * TBH], F32R)
    ip_s = em.sbuf("ip_s", [128, 8 * 1024], F32R)
    xp_s = em.sbuf("xp_s", [128, 4 * 33], F32R)
    cw_s = em.sbuf("cw_s", [128, 16], F32)
    cb_s = em.sbuf("cb_s", [128, 4], F32)
    dc_s = em.sbuf("dc_s", [128, 4], F32)
    xi_b = em.sbuf("xi_b", [128, 4 * TBH], F32)
    xc_b = em.sbuf("xc_b", [128, 4 * TB], F32R)
    g1_b = em.sbuf("g1_b", [128, 4 * TB], F32)
    g2_b = em.sbuf("g2_b", [128, 4 * TB], F32)
    tmp_b = em.sbuf("tmp_b", [128, 2 * TB], F32)
    xpp_b = em.sbuf("xpp_b", [33, TB], F32)
    p_mm = [em.psum(f"p_mm{i}", [128, 512]) for i in range(4)]
    p_xp = em.psum("p_xp", [33, 512])
    wc_b = [em.sbuf(f"wc_b{i}", [128, 4096], BF16) for i in range(2)]

    xT_v = xT.rearrange("(k p) t -> p k t", p=128)
    ipT_v = ipT.rearrange("(k p) n -> p k n", p=128)

    em.dma(ip_s[:].rearrange("p (k n) -> p k n", k=8), ipT_v, writes=["ip"])
    em.dma(xp_s[:], xpT, writes=["xpw"])
    em.dma(cw_s[:], convw, writes=["cw"])
    em.dma(cb_s[:], convb, writes=["cb"])
    em.dma(dc_s[:], dcol, writes=["dc"])

    xi_grps = [(0, 512), (512, 512), (1023, 4)]
    z_grps = [(H, 512), (H + 512, 512)]

    for ib in range(NB):
        t0 = ib * TB
        em.dma(xt_b[:].rearrange("p (k c) -> p k c", k=8),
               xT_v[:, :, t0:t0 + TBH], writes=["xt"])
        pi = 0
        for oc in range(8):
            grps = xi_grps if oc < 4 else z_grps
            for (c0, cn) in grps:
                slot = pi % 4
                pm = p_mm[slot]
                pi += 1
                pairs = [(ip_s[:, (k * 8 + oc) * 128:(k * 8 + oc) * 128 + 128],
                          xt_b[:, k * TBH + c0: k * TBH + c0 + cn])
                         for k in range(8)]
                emit_matmul_acc(em, pm[:, 0:cn], pairs,
                                reads=["ip", "xt"], writes=[("pm", slot)])
                if oc < 4:
                    dst = xi_b[:, oc * TBH + c0: oc * TBH + c0 + cn]
                    em.op("scalar",
                          lambda eng, dst=dst, pm=pm, cn=cn: eng.copy(out=dst, in_=pm[:, 0:cn]),
                          reads=[("pm", slot)], writes=[("xi", oc)])
                else:
                    dt_ = oc - 4
                    dst = g1_b[:, dt_ * TB + (c0 - H): dt_ * TB + (c0 - H) + cn]
                    em.op("scalar",
                          lambda eng, dst=dst, pm=pm, cn=cn: eng.activation(
                              out=dst, in_=pm[:, 0:cn], func=AF.Silu),
                          reads=[("pm", slot)], writes=[("g1", dt_)])
        for r in range(4):
            xiv = xi_b[:, r * TBH:(r + 1) * TBH]
            acc0 = tmp_b[:, 0:TB]
            acc1 = tmp_b[:, TB:2 * TB]
            em.op("vector",
                  lambda eng, xiv=xiv, acc0=acc0, r=r: eng.tensor_scalar(
                      out=acc0, in0=xiv[:, 0:TB], scalar1=cw_s[:, r * 4:r * 4 + 1],
                      scalar2=None, op0=ALU.mult),
                  reads=[("xi", r), "cw"], writes=["acc0"])
            for j in (1, 2, 3):
                src, dst_ = (acc0, acc1) if j % 2 else (acc1, acc0)
                em.op("vector",
                      lambda eng, xiv=xiv, src=src, dst_=dst_, r=r, j=j:
                      eng.scalar_tensor_tensor(
                          out=dst_, in0=xiv[:, j:j + TB],
                          scalar=cw_s[:, r * 4 + j:r * 4 + j + 1], in1=src,
                          op0=ALU.mult, op1=ALU.add),
                      reads=[("xi", r), "cw", "acc0" if j % 2 else "acc1"],
                      writes=["acc1" if j % 2 else "acc0"])
            em.op("scalar",
                  lambda eng, r=r, acc1=acc1: eng.activation(
                      out=xc_b[:, r * TB:(r + 1) * TB], in_=acc1,
                      func=AF.Silu, bias=cb_s[:, r:r + 1]),
                  reads=["acc1", "cb"], writes=[("xc", r)])
            em.op("vector",
                  lambda eng, r=r, acc0=acc0: eng.tensor_scalar(
                      out=acc0, in0=xc_b[:, r * TB:(r + 1) * TB].bitcast(F32),
                      scalar1=dc_s[:, r:r + 1], scalar2=None, op0=ALU.mult),
                  reads=[("xc", r), "dc"], writes=["acc0"])
            em.op("vector",
                  lambda eng, r=r, acc0=acc0: eng.tensor_tensor(
                      out=g2_b[:, r * TB:(r + 1) * TB], in0=acc0,
                      in1=g1_b[:, r * TB:(r + 1) * TB], op=ALU.mult),
                  reads=["acc0", ("g1", r)], writes=[("g2", r)])
        for (c0, cn) in ((0, 512), (512, 512)):
            pairs = [(xp_s[:, k * 33:(k + 1) * 33],
                      xc_b[:, k * TB + c0: k * TB + c0 + cn])
                     for k in range(4)]
            emit_matmul_acc(em, p_xp[:, 0:cn], pairs,
                            reads=["xpw", ("xc", 0), ("xc", 1), ("xc", 2), ("xc", 3)],
                            writes=["pxp"])
            em.op("scalar",
                  lambda eng, c0=c0, cn=cn: eng.copy(
                      out=xpp_b[:, c0:c0 + cn], in_=p_xp[:, 0:cn]),
                  reads=["pxp"], writes=["xpp"])
        em.dma(xc_o.rearrange("(r p) t -> p r t", p=128)[:, :, t0:t0 + TB],
               xc_b[:].rearrange("p (r t) -> p r t", r=4).bitcast(F32),
               reads=[("xc", r) for r in range(4)], writes=[], stream="o_xc")
        em.dma(g1_o.rearrange("(r p) t -> p r t", p=128)[:, :, t0:t0 + TB],
               g1_b[:].rearrange("p (r t) -> p r t", r=4),
               reads=[("g1", r) for r in range(4)], writes=[], stream="o_g1")
        em.dma(g2_o.rearrange("(r p) t -> p r t", p=128)[:, :, t0:t0 + TB],
               g2_b[:].rearrange("p (r t) -> p r t", r=4),
               reads=[("g2", r) for r in range(4)], writes=[], stream="o_g2")
        em.dma(xpp_o[:, t0:t0 + TB], xpp_b[:, :],
               reads=["xpp"], writes=[], stream="o_xpp")
    nrows = WCAST_A // 4096
    for ch in range(nrows // 128):
        sl = ch % 2
        em.dma(wc_b[sl][:], wsrc[ch * 128:(ch + 1) * 128, :],
               writes=[("wc", sl)], queue="gpsimd", stream=f"wc_{sl}")
        em.dma(wbf_o[ch * 128:(ch + 1) * 128, :], wc_b[sl][:],
               reads=[("wc", sl)], writes=[("wco", sl)], queue="gpsimd",
               stream=f"wco_{sl}")
    em.final_wait("gpsimd")
    return em.realize()


def prep_A_inputs(inputs):
    global _A_WFLAT
    w1 = np.asarray(inputs["L0_ffn_w1"])
    w2 = np.asarray(inputs["L0_ffn_w2"])
    w1T4 = f32(w1.transpose(0, 2, 1).reshape(8, 8, 128, 4096))
    w2T4 = f32(w2.transpose(0, 2, 1).reshape(8, 32, 128, 1024))
    _A_WFLAT = np.concatenate([w1T4.reshape(-1), w2T4.reshape(-1)])
    x = np.asarray(inputs["L0_attn_x"])
    ipw = np.asarray(inputs["L0_attn_in_proj_w"])
    cw = np.asarray(inputs["L0_attn_conv_w"])
    cb = np.asarray(inputs["L0_attn_conv_b"])
    xpw = np.asarray(inputs["L0_attn_x_proj_w"])
    Dv = np.asarray(inputs["L0_attn_D"])
    ipT_full = f32(ipw.T)
    xpT_full = f32(xpw.T)
    maps = []
    for c in range(N_CORES):
        b, ds = divmod(c, 4)
        d0 = ds * DSL
        ipT = np.concatenate([ipT_full[:, d0:d0 + DSL],
                              ipT_full[:, D_INNER + d0:D_INNER + d0 + DSL]], axis=1)
        convw = cw[d0:d0 + DSL, 0, :].reshape(4, 128, 4).transpose(1, 0, 2).reshape(128, 16)
        convb = cb[d0:d0 + DSL].reshape(4, 128).T
        dc = Dv[d0:d0 + DSL].reshape(4, 128).T
        xpT = xpT_full[d0:d0 + DSL].reshape(4, 128, 33).transpose(1, 0, 2).reshape(128, 4 * 33)
        wflat = _A_WFLAT
        nper = WCAST_A
        maps.append({
            "wsrc": wflat[c * nper:(c + 1) * nper].reshape(-1, 4096),
            "xT": f32(np.concatenate([np.zeros((D_MODEL, 3), np.float32),
                                      x[b].T.astype(np.float32)], axis=1)),
            "ipT": f32(ipT),
            "convw": f32(convw),
            "convb": f32(convb),
            "dcol": f32(dc),
            "xpT": f32(xpT),
        })
    return maps



def build_S():
    em = Emitter("S")
    xc_i = em.dram("xc", [DSL, SEQ], F32)
    g1_i = em.dram("g1", [DSL, SEQ], F32)
    g2_i = em.dram("g2", [DSL, SEQ], F32)
    xpp4 = em.dram("xpp4", [4, 33, SEQ], F32)
    alog = em.dram("alog", [128, 64], F32)
    dtw = em.dram("dtw", [128, 4], F32)
    dtb = em.dram("dtb", [128, 4], F32)
    opT = em.dram("opT", [128, 4 * 1024], F32R)
    repE = em.dram("repE", [33, 33 * 128], F32R)
    po_o = em.dram("po", [D_MODEL, SEQ], F32, kind="ExternalOutput")
    wsrc = em.dram("wsrc", [WCAST_S // 4096, 4096], F32)
    wbf_o = em.dram("wbf", [WCAST_S // 4096, 4096], BF16, kind="ExternalOutput")

    TC = 1024
    NCH = SEQ // TC
    NG = TC // 512

    xp_s = em.sbuf("xp_s", [33, TC], F32R)
    xp_t = em.sbuf("xp_t", [33, TC], F32)
    alog_s = em.sbuf("alog_s", [128, 64], F32)
    negA = em.sbuf("negA", [128, 64], F32)
    dtw_s = em.sbuf("dtw_s", [128, 4], F32)
    dtb_s = em.sbuf("dtb_s", [128, 4], F32)
    opT_s = em.sbuf("opT_s", [128, 4 * 1024], F32R)
    repE_s = em.sbuf("repE_s", [33, 33 * 128], F32R)
    xc_c = em.sbuf("xc_c", [128, 4 * TC], F32)
    yg_c = em.sbuf("yg_c", [128, 4 * TC], F32R)
    dt_c = em.sbuf("dt_c", [128, 4 * TC], F32)
    dtr_c = em.sbuf("dtr_c", [128, TC], F32)
    u_c = em.sbuf("u_c", [128, 4 * TC], BF16)
    g1_c = em.sbuf("g1_c", [128, 4 * TC], BF16)
    g2_c = em.sbuf("g2_c", [128, 4 * TC], BF16)
    brep = em.sbuf("brep", [128, TC], BF16)
    crep = em.sbuf("crep", [128, TC], BF16)
    da_c = [em.sbuf(f"da{i}", [128, TC], BF16) for i in range(2)]
    dbx_c = [em.sbuf(f"dbx{i}", [128, TC], BF16) for i in range(2)]
    h_c = [em.sbuf(f"h{i}", [128, TC], BF16) for i in range(2)]
    t_c = [em.sbuf(f"t{i}", [128, TC], BF16) for i in range(2)]
    y_c = em.sbuf("y_c", [128, 4 * TC], F32)
    st_s = em.sbuf("st_s", [128, 64], F32)
    sp_t = em.sbuf("sp_t", [128, 2 * TC], F32)
    po_s = em.sbuf("po_s", [128, 8 * 512], F32)
    p_rep = [em.psum(f"p_rep{i}", [128, 512]) for i in range(2)]
    p_po = [em.psum(f"p_po{i}", [128, 512]) for i in range(2)]
    wc_b = [em.sbuf(f"wc_b{i}", [128, 4096], BF16) for i in range(2)]
    ygv = yg_c[:]

    em.dma(alog_s[:], alog, writes=["alog"])
    em.dma(dtw_s[:], dtw, writes=["dtw"])
    em.dma(dtb_s[:], dtb, writes=["dtb"])
    em.dma(opT_s[:], opT, writes=["opw"])
    em.dma(repE_s[:], repE, writes=["repE"])
    em.op("scalar", lambda eng: eng.activation(out=negA[:], in_=alog_s[:], func=AF.Exp),
          reads=["alog"], writes=["negA"])
    em.op("vector", lambda eng: eng.tensor_scalar_mul(out=negA[:], in0=negA[:], scalar1=-1.0),
          reads=["negA"], writes=["negA"])

    for ci in range(NCH):
        t0 = ci * TC
        em.dma(xp_t[:], xpp4[0, :, t0:t0 + TC], writes=["xpt"], stream="l_xp")
        em.op("vector", lambda eng: eng.tensor_copy(out=xp_s[:], in_=xp_t[:]),
              reads=["xpt"], writes=["xps"])
        for r in range(1, 4):
            em.dma(xp_t[:], xpp4[r, :, t0:t0 + TC], reads=["xps"], writes=["xpt"], stream="l_xp")
            em.op("vector",
                  lambda eng: eng.tensor_tensor(out=xp_s[:],
                                                in0=xp_s[:].bitcast(F32),
                                                in1=xp_t[:], op=ALU.add),
                  reads=["xpt", "xps"], writes=["xps"])
        em.dma(xc_c[:].rearrange("p (r t) -> p r t", r=4),
               xc_i.rearrange("(r p) t -> p r t", p=128)[:, :, t0:t0 + TC],
               writes=["xcc"])
        em.dma(g1_c[:].rearrange("p (r t) -> p r t", r=4),
               g1_i.rearrange("(r p) t -> p r t", p=128)[:, :, t0:t0 + TC],
               writes=["g1c"], queue="gpsimd")
        em.dma(g2_c[:].rearrange("p (r t) -> p r t", r=4),
               g2_i.rearrange("(r p) t -> p r t", p=128)[:, :, t0:t0 + TC],
               writes=["g2c"], queue="gpsimd")
        for g in range(NG):
            pr = p_rep[g % 2]
            em.op("tensor",
                  lambda eng, g=g, pr=pr: eng.matmul(
                      out=pr[:, 0:512], lhsT=repE_s[:, 32 * 128:33 * 128],
                      rhs=xp_s[:, g * 512: g * 512 + 512],
                      start=True, stop=True),
                  reads=["repE", "xps"], writes=[("prep", g % 2)])
            em.op("scalar",
                  lambda eng, g=g, pr=pr: eng.copy(
                      out=dtr_c[:, g * 512:(g + 1) * 512], in_=pr[:, 0:512]),
                  reads=[("prep", g % 2)], writes=["dtr"])
        for r in range(4):
            a_ = sp_t[:, 0:TC]
            b_ = sp_t[:, TC:2 * TC]
            em.op("scalar",
                  lambda eng, r=r, a_=a_: eng.activation(
                      out=a_, in_=dtr_c[:], func=AF.Abs,
                      scale=dtw_s[:, r:r + 1], bias=dtb_s[:, r:r + 1]),
                  reads=["dtr", "dtw", "dtb"], writes=["spA"])
            em.op("scalar",
                  lambda eng, a_=a_: eng.activation(out=a_, in_=a_, func=AF.Exp, scale=-1.0),
                  reads=["spA"], writes=["spA"])
            em.op("scalar",
                  lambda eng, a_=a_: eng.activation(out=a_, in_=a_, func=AF.Ln, bias=1.0),
                  reads=["spA"], writes=["spA"])
            em.op("scalar",
                  lambda eng, r=r, b_=b_: eng.activation(
                      out=b_, in_=dtr_c[:], func=AF.Relu,
                      scale=dtw_s[:, r:r + 1], bias=dtb_s[:, r:r + 1]),
                  reads=["dtr", "dtw", "dtb"], writes=["spB"])
            em.op("vector",
                  lambda eng, r=r, a_=a_, b_=b_: eng.tensor_tensor(
                      out=dt_c[:, r * TC:(r + 1) * TC], in0=a_, in1=b_, op=ALU.add),
                  reads=["spA", "spB"], writes=[("dt", r)])
            em.op("vector",
                  lambda eng, r=r: eng.tensor_tensor(
                      out=u_c[:, r * TC:(r + 1) * TC],
                      in0=dt_c[:, r * TC:(r + 1) * TC],
                      in1=xc_c[:, r * TC:(r + 1) * TC], op=ALU.mult),
                  reads=[("dt", r), "xcc"], writes=[("u", r)])
            em.op("vector",
                  lambda eng, r=r: eng.memset(y_c[:, r * TC:(r + 1) * TC], 0.0),
                  writes=[("y", r)])
        for n in range(D_STATE):
            for which, dst, ecol in (("brep", brep, n), ("crep", crep, 16 + n)):
                for g in range(NG):
                    pr = p_rep[g % 2]
                    em.op("tensor",
                          lambda eng, g=g, pr=pr, ecol=ecol: eng.matmul(
                              out=pr[:, 0:512],
                              lhsT=repE_s[:, ecol * 128:(ecol + 1) * 128],
                              rhs=xp_s[:, g * 512: g * 512 + 512],
                              start=True, stop=True),
                          reads=["repE", "xps"], writes=[("prep", g % 2)])
                    em.op("scalar",
                          lambda eng, g=g, pr=pr, dst=dst: eng.copy(
                              out=dst[:, g * 512:(g + 1) * 512], in_=pr[:, 0:512]),
                          reads=[("prep", g % 2)], writes=[which])
            for r in range(4):
                sl = (n * 4 + r) % 2
                col = r * 16 + n
                em.op("scalar",
                      lambda eng, r=r, n=n, sl=sl: eng.activation(
                          out=da_c[sl][:], in_=dt_c[:, r * TC:(r + 1) * TC],
                          func=AF.Exp, scale=negA[:, r * 16 + n:r * 16 + n + 1]),
                      reads=[("dt", r), "negA"], writes=[("da", sl)])
                em.op("vector",
                      lambda eng, r=r, sl=sl: eng.tensor_tensor(
                          out=dbx_c[sl][:], in0=u_c[:, r * TC:(r + 1) * TC],
                          in1=brep[:], op=ALU.mult),
                      reads=[("u", r), "brep"], writes=[("dbx", sl)])
                init = 0.0 if ci == 0 else st_s[:, col:col + 1]
                extra = [] if ci == 0 else [("st", col)]
                em.op("vector",
                      lambda eng, sl=sl, init=init: eng.tensor_tensor_scan(
                          out=h_c[sl][:], data0=da_c[sl][:], data1=dbx_c[sl][:],
                          initial=init, op0=ALU.mult, op1=ALU.add),
                      reads=[("da", sl), ("dbx", sl)] + extra, writes=[("h", sl)])
                em.op("gpsimd",
                      lambda eng, sl=sl, col=col: eng.tensor_copy(
                          out=st_s[:, col:col + 1], in_=h_c[sl][:, TC - 1:TC]),
                      reads=[("h", sl)], writes=[("st", col)])
                em.op("vector",
                      lambda eng, sl=sl: eng.tensor_tensor(
                          out=t_c[sl][:], in0=h_c[sl][:], in1=crep[:], op=ALU.mult),
                      reads=[("h", sl), "crep"], writes=[("t", sl)])
                em.op("gpsimd",
                      lambda eng, r=r, sl=sl: eng.tensor_tensor(
                          out=y_c[:, r * TC:(r + 1) * TC],
                          in0=y_c[:, r * TC:(r + 1) * TC],
                          in1=t_c[sl][:], op=ALU.add),
                      reads=[("t", sl), ("y", r)], writes=[("y", r)])
        for r in range(4):
            em.op("vector",
                  lambda eng, r=r: eng.tensor_tensor(
                      out=y_c[:, r * TC:(r + 1) * TC],
                      in0=y_c[:, r * TC:(r + 1) * TC],
                      in1=g1_c[:, r * TC:(r + 1) * TC], op=ALU.mult),
                  reads=[("y", r), "g1c"], writes=[("y", r)])
            em.op("vector",
                  lambda eng, r=r: eng.tensor_tensor(
                      out=ygv[:, r * TC:(r + 1) * TC],
                      in0=y_c[:, r * TC:(r + 1) * TC],
                      in1=g2_c[:, r * TC:(r + 1) * TC], op=ALU.add),
                  reads=[("y", r), "g2c"], writes=[("yg", r)])
        for g in range(NG):
            for oc in range(8):
                pp = p_po[oc % 2]
                pairs = [(opT_s[:, r * 1024 + oc * 128: r * 1024 + oc * 128 + 128],
                          ygv[:, r * TC + g * 512: r * TC + g * 512 + 512])
                         for r in range(4)]
                emit_matmul_acc(em, pp[:, 0:512], pairs,
                                reads=["opw"] + [("yg", r) for r in range(4)],
                                writes=[("ppo", oc % 2)])
                em.op("scalar",
                      lambda eng, oc=oc, pp=pp: eng.copy(
                          out=po_s[:, oc * 512:(oc + 1) * 512], in_=pp[:, 0:512]),
                      reads=[("ppo", oc % 2)], writes=[("po", oc)])
            em.dma(po_o.rearrange("(o p) t -> p o t", p=128)[:, :, t0 + g * 512: t0 + g * 512 + 512],
                   po_s[:].rearrange("p (o t) -> p o t", o=8),
                   reads=[("po", oc) for oc in range(8)], stream="o_po")
    nrows = WCAST_S // 4096
    for ch in range((nrows + 127) // 128):
        sl = ch % 2
        rr = min(128, nrows - ch * 128)
        em.dma(wc_b[sl][0:rr, :], wsrc[ch * 128:ch * 128 + rr, :],
               writes=[("wc", sl)], queue="gpsimd", stream=f"wc_{sl}")
        em.dma(wbf_o[ch * 128:ch * 128 + rr, :], wc_b[sl][0:rr, :],
               reads=[("wc", sl)], writes=[("wco", sl)], queue="gpsimd",
               stream=f"wco_{sl}")
    em.final_wait("gpsimd")
    return em.realize()


def prep_S_inputs(inputs, a_results):
    global _S_WFLAT
    ew1 = np.asarray(inputs["L1_ffn_e_w1"])
    ew2 = np.asarray(inputs["L1_ffn_e_w2"])
    Wo_attn = np.asarray(inputs["L1_attn_Wout"])
    ew14 = f32(ew1.reshape(8, 8, 128, 4096))
    ew24 = f32(ew2.reshape(8, 32, 128, 1024))
    woutT = f32(Wo_attn.reshape(8, 128, 1024))
    _S_WFLAT = np.concatenate([ew14.reshape(-1), ew24.reshape(-1), woutT.reshape(-1)])
    alog_f = np.asarray(inputs["L0_attn_A_log"])
    dtw_f = np.asarray(inputs["L0_attn_dt_proj_w"])
    dtb_f = np.asarray(inputs["L0_attn_dt_proj_b"])
    opw = np.asarray(inputs["L0_attn_out_proj_w"])
    opT_full = f32(opw.T)
    repE = np.zeros((33, 33 * 128), np.float32)
    for j in range(16):
        repE[1 + j, j * 128:(j + 1) * 128] = 1.0
    for j in range(16):
        repE[17 + j, (16 + j) * 128:(17 + j) * 128] = 1.0
    repE[0, 32 * 128:33 * 128] = 1.0
    maps = []
    for c in range(N_CORES):
        b, ds = divmod(c, 4)
        d0 = ds * DSL
        xpp4 = np.stack([a_results[b * 4 + k]["xpp"] for k in range(4)], axis=0)
        alog = alog_f[d0:d0 + DSL].reshape(4, 128, 16).transpose(1, 0, 2).reshape(128, 64)
        nper = WCAST_S
        maps.append({
            "wsrc": _S_WFLAT[c * nper:(c + 1) * nper].reshape(-1, 4096),
            "xc": a_results[c]["xc"],
            "g1": a_results[c]["g1"],
            "g2": a_results[c]["g2"],
            "xpp4": f32(xpp4),
            "alog": f32(alog),
            "dtw": f32(dtw_f[d0:d0 + DSL, 0].reshape(4, 128).T),
            "dtb": f32(dtb_f[d0:d0 + DSL].reshape(4, 128).T),
            "opT": f32(opT_full[d0:d0 + DSL].reshape(4, 128, 1024).transpose(1, 0, 2).reshape(128, 4 * 1024)),
            "repE": repE,
        })
    return maps




def make_emit_ln(em, PP, ONESC, ONES1, SQT, stat_m, stat_r, stat_t, REPS,
                 TMPA, TMPB, TH):
    def emit_ln(src, src_key, g_col=None, b_col=None, gb_key=None,
                dst=None, dst_key=None, dst2=None, dst2_key=None):
        def mm_s1(eng):
            ins = None
            for k in range(8):
                ins = eng.matmul(out=PP[6][0:1, 0:TH], lhsT=ONESC[:],
                                 rhs=src[:, k * TH:(k + 1) * TH],
                                 start=(k == 0), stop=(k == 7))
            return ins
        em.op("tensor", mm_s1, reads=["onesc", src_key], writes=[("pp", 6)])
        for k in range(8):
            em.op("scalar",
                  lambda eng, k=k: eng.activation(
                      out=SQT[:], in_=src[:, k * TH:(k + 1) * TH].bitcast(F32),
                      func=AF.Square),
                  reads=[src_key], writes=["sqt"])
            em.op("tensor",
                  lambda eng, k=k: eng.matmul(
                      out=PP[7][0:1, 0:TH], lhsT=ONESC[:], rhs=SQT[:],
                      start=(k == 0), stop=(k == 7)),
                  reads=["onesc", "sqt"], writes=[("pp", 7)])
        em.op("vector", lambda eng: eng.tensor_scalar_mul(
                  out=stat_m, in0=PP[6][0:1, 0:TH], scalar1=1.0 / D_MODEL),
              reads=[("pp", 6)], writes=["stat_m"])
        em.op("vector", lambda eng: eng.tensor_scalar_mul(
                  out=stat_r, in0=PP[7][0:1, 0:TH], scalar1=1.0 / D_MODEL),
              reads=[("pp", 7)], writes=["stat_r"])
        em.op("vector",
              lambda eng: eng.tensor_tensor(out=stat_t, in0=stat_m.bitcast(F32),
                                            in1=stat_m.bitcast(F32), op=ALU.mult),
              reads=["stat_m"], writes=["stat_t"])
        em.op("vector",
              lambda eng: eng.tensor_tensor(out=stat_r, in0=stat_r.bitcast(F32),
                                            in1=stat_t.bitcast(F32), op=ALU.subtract),
              reads=["stat_r", "stat_t"], writes=["stat_r"])
        em.op("vector",
              lambda eng: eng.tensor_scalar_add(out=stat_r, in0=stat_r.bitcast(F32),
                                                scalar1=EPS),
              reads=["stat_r"], writes=["stat_r"])
        em.op("scalar",
              lambda eng: eng.activation(out=stat_r, in_=stat_r.bitcast(F32),
                                         func=AF.Ln),
              reads=["stat_r"], writes=["stat_r"])
        em.op("vector",
              lambda eng: eng.tensor_scalar_mul(out=stat_r, in0=stat_r.bitcast(F32),
                                                scalar1=-0.5),
              reads=["stat_r"], writes=["stat_r"])
        em.op("scalar",
              lambda eng: eng.activation(out=stat_r, in_=stat_r.bitcast(F32),
                                         func=AF.Exp),
              reads=["stat_r"], writes=["stat_r"])
        em.op("vector",
              lambda eng: eng.tensor_tensor(out=stat_t, in0=stat_m.bitcast(F32),
                                            in1=stat_r.bitcast(F32), op=ALU.mult),
              reads=["stat_m", "stat_r"], writes=["stat_t"])
        for col, srow, rkey in ((0, stat_r, "stat_r"), (1, stat_t, "stat_t")):
            em.op("tensor",
                  lambda eng, col=col, srow=srow: eng.matmul(
                      out=PP[6 + col][:, 0:TH], lhsT=ONES1[:], rhs=srow,
                      start=True, stop=True),
                  reads=["ones1", rkey], writes=[("pp", 6 + col)])
            em.op("scalar",
                  lambda eng, col=col: eng.copy(
                      out=REPS[:, col * TH:(col + 1) * TH],
                      in_=PP[6 + col][:, 0:TH]),
                  reads=[("pp", 6 + col)], writes=[("reps", col)])
        for k in range(8):
            em.op("vector",
                  lambda eng, k=k: eng.tensor_tensor(
                      out=TMPA, in0=src[:, k * TH:(k + 1) * TH].bitcast(F32),
                      in1=REPS[:, 0:TH], op=ALU.mult),
                  reads=[src_key, ("reps", 0)], writes=["tmpa"])
            if g_col is None:
                em.op("vector",
                      lambda eng, k=k: eng.tensor_tensor(
                          out=dst[:, k * TH:(k + 1) * TH], in0=TMPA,
                          in1=REPS[:, TH:2 * TH], op=ALU.subtract),
                      reads=["tmpa", ("reps", 1)], writes=[dst_key])
            else:
                em.op("vector",
                      lambda eng, k=k: eng.tensor_tensor(
                          out=TMPB, in0=TMPA,
                          in1=REPS[:, TH:2 * TH], op=ALU.subtract),
                      reads=["tmpa", ("reps", 1)], writes=["tmpb"])
                em.op("vector",
                      lambda eng, k=k: eng.tensor_scalar(
                          out=dst[:, k * TH:(k + 1) * TH], in0=TMPB,
                          scalar1=g_col[:, k:k + 1], scalar2=b_col[:, k:k + 1],
                          op0=ALU.mult, op1=ALU.add),
                      reads=["tmpb", gb_key], writes=[dst_key])
            if dst2 is not None:
                em.op("scalar",
                      lambda eng, k=k: eng.copy(
                          out=dst2[:, k * TH:(k + 1) * TH],
                          in_=dst[:, k * TH:(k + 1) * TH].bitcast(F32)),
                      reads=[dst_key], writes=[dst2_key])
    return emit_ln



def build_B():
    em = Emitter("B")
    TH = 512
    NQ = 4
    TPC = TH * NQ
    NEGB = -30000.0
    xT = em.dram("xT", [D_MODEL, TPC], F32R)
    po4 = em.dram("po4", [4, D_MODEL, TPC], F32)
    gwT = em.dram("gwT", [128, 8 * N_EXP], F32)
    gbc = em.dram("gbc", [N_EXP, 1], F32)
    w1T4 = em.dram("w1T4", [N_EXP, 8, 128, D_FF], BF16)
    b1r = em.dram("b1r", [N_EXP, 1, D_FF], F32)
    w2T4 = em.dram("w2T4", [N_EXP, 32, 128, D_MODEL], BF16)
    b2sd = em.dram("b2sd", [16, D_MODEL], F32)
    repE16d = em.dram("repE16d", [16, 16 * 128], F32)
    onescol = em.dram("onescol", [128, 1], F32R)
    ones1 = em.dram("ones1", [1, 128], F32R)
    aw = em.dram("aw", [8, 128, D_MODEL + D_LATENT], F32R)
    awbc = em.dram("awbc", [128, 10], F32)
    wupT = em.dram("wupT", [2, 128, 2 * D_MODEL], F32R)
    bupc = em.dram("bupc", [128, 16], F32)
    woutT = em.dram("woutT", [8, 128, D_MODEL], BF16)
    boutc = em.dram("boutc", [128, 8], F32)
    ropeC = em.dram("ropeC", [128, TPC], F32)
    ropeS = em.dram("ropeS", [128, TPC], F32)
    ropeCk = em.dram("ropeCk", [128, TPC], F32)
    ropeSk = em.dram("ropeSk", [128, TPC], F32)
    permq = em.dram("permq", [128, 128], F32R)
    permkv = em.dram("permkv", [128, 128], F32R)
    augkd = em.dram("augkd", [9, 128], F32)
    augqd = em.dram("augqd", [9, 128], F32)
    identd = em.dram("identd", [128, 128], F32)
    ln1gbd = em.dram("ln1gbd", [128, 16], F32)
    wqkv = em.dram("wqkv", [8, 128, 3 * D_MODEL], F32R)
    bqkvc = em.dram("bqkvc", [128, 24], F32)
    a_o = em.dram("aT", [D_MODEL, TPC], F32, kind="ExternalOutput")
    q_o = em.dram("qT", [D_MODEL, TPC], F32, kind="ExternalOutput")
    k_o = em.dram("kT", [D_MODEL, TPC], F32, kind="ExternalOutput")
    v_o = em.dram("vT", [D_MODEL, TPC], F32, kind="ExternalOutput")
    qtok = em.dram("qtok", [TPC * 16 * 64], BF16, kind="Internal")
    kvtok = em.dram("kvtok", [TPC * 16 * 128], BF16, kind="Internal")
    atok = em.dram("atok", [TPC * 1024], BF16, kind="Internal")
    wmscr = em.dram("wmscr", [16, 4, 128], BF16, kind="Internal")

    R = em.sbuf("R", [128, 8 * TH], F32R)
    H1 = em.sbuf("H1", [128, 8 * TH], F32R)
    XIB = em.sbuf("XIB", [128, 8 * TH], BF16)
    HBUF = em.sbuf("HBUF", [128, 32 * TH], BF16)
    W1S = [em.sbuf(f"W1S{i}", [128, 8 * TH], BF16) for i in range(3)]
    W2S = [em.sbuf(f"W2S{i}", [128, 4 * 1024], BF16) for i in range(3)]
    AWS = [em.sbuf(f"AWS{i}", [128, 8 * 128], F32R) for i in range(2)]
    PF = em.sbuf("PF", [128, 3 * TH], F32R)
    PS_ = em.sbuf("PS_", [128, 2 * TH], F32)
    REPS = em.sbuf("REPS", [128, 2 * TH], F32)
    SQT = em.sbuf("SQT", [128, TH], F32R)
    ROWS = em.sbuf("ROWS", [1, 3 * TH], F32R)
    ONESC = em.sbuf("ONESC", [128, 1], F32R)
    ONES1 = em.sbuf("ONES1", [1, 128], F32R)
    B1S = em.sbuf("B1S", [1, D_FF], BF16)
    MROW = em.sbuf("MROW", [1, TH], BF16)
    GW = em.sbuf("GW", [128, 8 * N_EXP], F32)
    GBC = em.sbuf("GBC", [N_EXP, 1], F32)
    B2S = em.sbuf("B2S", [16, D_MODEL], BF16)
    RE16 = em.sbuf("RE16", [16, 16 * 128], BF16)
    WM16 = em.sbuf("WM16", [16, 4 * 128], BF16)
    MREP = em.sbuf("MREP", [128, TH], BF16)
    WREP = em.sbuf("WREP", [128, TH], BF16)
    AWBC = em.sbuf("AWBC", [128, 10], F32)
    BUPC = em.sbuf("BUPC", [128, 16], F32)
    BOUTC = em.sbuf("BOUTC", [128, 8], F32)
    BQKVC = em.sbuf("BQKVC", [128, 24], F32)
    RC = em.sbuf("RC", [128, TH], BF16)
    RS = em.sbuf("RS", [128, TH], BF16)
    RCK = em.sbuf("RCK", [128, TH], BF16)
    RSK = em.sbuf("RSK", [128, TH], BF16)
    PERMQ = em.sbuf("PERMQ", [128, 128], F32R)
    PERMKV = em.sbuf("PERMKV", [128, 128], F32R)
    IDENT = em.sbuf("IDENT", [128, 128], BF16)
    IDF = em.sbuf("IDF", [128, 128], F32)
    LN1GB = em.sbuf("LN1GB", [128, 16], F32)
    LGT = em.sbuf("LGT", [8, TH], F32)
    LTOK = em.sbuf("LTOK", [128, 4 * 48], F32)
    WMTOK = em.sbuf("WMTOK", [128, 16], BF16)
    QC = em.sbuf("QC", [128, TH], F32R)
    CBF = em.sbuf("CBF", [128, TH], BF16)
    COLL = em.sbuf("COLL", [128, 4 * 128], BF16)
    RQT = [em.sbuf(f"RQT{i}", [73, 128], BF16) for i in range(2)]
    RKT = [em.sbuf(f"RKT{i}", [73, 128], BF16) for i in range(2)]
    RKV = [em.sbuf(f"RKV{i}", [128, 128], BF16) for i in range(2)]
    RQL = [em.sbuf(f"RQL{i}", [128, 64], BF16) for i in range(2)]
    RV = em.sbuf("RV", [128, 68], BF16)
    RP = [em.sbuf(f"RP{i}", [128, 128], BF16) for i in range(2)]
    RO = [em.sbuf(f"RO{i}", [128, 64], BF16) for i in range(2)]
    RREC = em.sbuf("RREC", [128, 5], F32)
    NEG20 = em.sbuf("NEG20", [128, 1], F32)
    ATL = [em.sbuf(f"ATL{i}", [128, 128], BF16) for i in range(2)]
    PP = [em.psum(f"pp{i}", [128, 512]) for i in range(8)]
    LAT = PF[:, 0:2 * TH]
    TMPA = PS_[:, 0:TH]
    TMPB = PS_[:, TH:2 * TH]
    stat_m = ROWS[0:1, 0:TH]
    stat_r = ROWS[0:1, TH:2 * TH]
    stat_t = ROWS[0:1, 2 * TH:3 * TH]

    for t, d, key in ((GW, gwT, "gw"), (GBC, gbc, "gbc"), (ONESC, onescol, "onesc"),
                      (ONES1, ones1, "ones1"), (AWBC, awbc, "awbc"),
                      (BUPC, bupc, "bupc"), (BOUTC, boutc, "boutc"),
                      (BQKVC, bqkvc, "bqkvc"), (PERMQ, permq, "permq"),
                      (PERMKV, permkv, "permkv"), (LN1GB, ln1gbd, "ln1gb"),
                      (IDF, identd, "idf")):
        em.dma(t[:], d, writes=[key])
    for t, d, key in ((B2S, b2sd, "b2s"), (RE16, repE16d, "re16"),
                      (IDENT, identd, "ident")):
        em.dma(t[:], d, writes=[key], queue="gpsimd")
    for i in range(2):
        em.dma(RQT[i][64:73, :], augqd, writes=[("augq", i)], queue="gpsimd",
               stream="aug_l")
        em.dma(RKT[i][64:73, :], augkd, writes=[("augk", i)], queue="gpsimd",
               stream="aug_l")
    em.op("vector", lambda eng: eng.memset(RV[:, 64:68], 1.0), writes=["vones"])
    em.op("vector", lambda eng: eng.memset(NEG20[:], -20.0), writes=["neg20"])

    emit_ln = make_emit_ln(em, PP, ONESC, ONES1, SQT, stat_m,
                           stat_r, stat_t, REPS, TMPA, TMPB, TH)

    for qi in range(NQ):
        t0 = qi * TH
        em.dma(H1[:].rearrange("p (k t) -> p k t", k=8),
               xT.rearrange("(k p) t -> p k t", p=128)[:, :, t0:t0 + TH],
               writes=["H1"])
        for r4 in range(4):
            em.dma(XIB[:].bitcast(F32)[:, 0:4 * TH].rearrange("p (k t) -> p k t", k=4),
                   po4[r4].rearrange("(k p) t -> p k t", p=128)[:, 0:4, t0:t0 + TH],
                   writes=[("xib", 0), ("xib", 1), ("xib", 2), ("xib", 3)],
                   stream="po_l")
            em.dma(HBUF[:].bitcast(F32)[:, 0:4 * TH].rearrange("p (k t) -> p k t", k=4),
                   po4[r4].rearrange("(k p) t -> p k t", p=128)[:, 4:8, t0:t0 + TH],
                   writes=[("hb", 0), ("hb", 1), ("hb", 2), ("hb", 3)],
                   stream="po_l2")
            for k in range(8):
                buf = XIB if k < 4 else HBUF
                kk = k % 4
                em.op("vector",
                      lambda eng, k=k, buf=buf, kk=kk: eng.tensor_tensor(
                          out=H1[:, k * TH:(k + 1) * TH],
                          in0=H1[:, k * TH:(k + 1) * TH].bitcast(F32),
                          in1=buf[:].bitcast(F32)[:, kk * TH:(kk + 1) * TH],
                          op=ALU.add),
                      reads=[("xib", kk) if k < 4 else ("hb", kk), "H1"],
                      writes=["H1"])
        emit_ln(H1, "H1", dst=R, dst_key="R", dst2=H1, dst2_key="H1")
        def mm_gate(eng):
            ins = None
            for k in range(8):
                ins = eng.matmul(out=PP[5][0:8, 0:TH],
                                 lhsT=GW[:, k * 8:(k + 1) * 8],
                                 rhs=R[:, k * TH:(k + 1) * TH].bitcast(F32),
                                 start=(k == 0), stop=(k == 7))
            return ins
        em.op("tensor", mm_gate, reads=["gw", "R"], writes=[("pp", 5)])
        em.op("scalar",
              lambda eng: eng.activation(out=LGT[:], in_=PP[5][0:8, 0:TH],
                                         func=AF.Identity, bias=GBC[:, 0:1]),
              reads=[("pp", 5), "gbc"], writes=["lgt"])
        for tt in range(4):
            em.op("tensor",
                  lambda eng, tt=tt: eng.transpose(
                      out=PP[4][0:128, 0:8], in_=LGT[0:8, tt * 128:(tt + 1) * 128],
                      identity=IDF[0:8, 0:8]),
                  reads=["lgt", "idf"], writes=[("pp", 4)])
            base = tt * 48
            lg = LTOK[:, base:base + 8]
            sc_ = LTOK[:, base + 8:base + 16]
            eqc = LTOK[:, base + 16:base + 24]
            msk = LTOK[:, base + 24:base + 32]
            sel = LTOK[:, base + 32:base + 40]
            suf = LTOK[:, base + 40:base + 48]
            em.op("vector",
                  lambda eng, lg=lg: eng.tensor_copy(out=lg, in_=PP[4][0:128, 0:8]),
                  reads=[("pp", 4)], writes=[("lg", tt)])
            em.op("vector",
                  lambda eng, lg=lg: eng.reduce_max(
                      out=RREC[:, 0:1], in_=lg, axis=AX.X, negate=True),
                  reads=[("lg", tt)], writes=["nmx"])
            em.op("scalar",
                  lambda eng, lg=lg, sc_=sc_: eng.activation(
                      out=sc_, in_=lg, func=AF.Exp, bias=RREC[:, 0:1],
                      accum_out=RREC[:, 1:2]),
                  reads=[("lg", tt), "nmx"], writes=["eg", "es"])
            em.op("vector",
                  lambda eng: eng.reciprocal(out=RREC[:, 2:3], in_=RREC[:, 1:2]),
                  reads=["es"], writes=["rs"])
            em.op("vector",
                  lambda eng, sc_=sc_: eng.tensor_scalar_mul(
                      out=sc_, in0=sc_, scalar1=RREC[:, 2:3]),
                  reads=["eg", "rs"], writes=[("sc", tt)])
            em.op("vector",
                  lambda eng, lg=lg, eqc=eqc: eng.tensor_scalar(
                      out=eqc, in0=lg, scalar1=RREC[:, 0:1], scalar2=0.0,
                      op0=ALU.add, op1=ALU.is_ge),
                  reads=[("lg", tt), "nmx"], writes=[("eq", tt)])
            em.op("vector",
                  lambda eng, lg=lg, eqc=eqc, msk=msk: eng.scalar_tensor_tensor(
                      out=msk, in0=eqc, scalar=NEGB, in1=lg,
                      op0=ALU.mult, op1=ALU.add),
                  reads=[("eq", tt), ("lg", tt)], writes=[("msk", tt)])
            em.op("vector",
                  lambda eng, msk=msk: eng.reduce_max(
                      out=RREC[:, 3:4], in_=msk, axis=AX.X, negate=True),
                  reads=[("msk", tt)], writes=["nthr"])
            em.op("vector",
                  lambda eng, lg=lg, sel=sel: eng.tensor_scalar(
                      out=sel, in0=lg, scalar1=RREC[:, 3:4], scalar2=0.0,
                      op0=ALU.add, op1=ALU.is_ge),
                  reads=[("lg", tt), "nthr"], writes=[("sel", tt)])
            em.op("vector",
                  lambda eng, suf=suf: eng.memset(suf[:, 7:8], 1.0),
                  writes=[("suf", tt, 7)])
            em.op("vector",
                  lambda eng, sel=sel, eqc=eqc: eng.tensor_scalar(
                      out=eqc, in0=sel, scalar1=-1.0, scalar2=1.0,
                      op0=ALU.mult, op1=ALU.add),
                  reads=[("sel", tt)], writes=[("eq", tt)])
            for j in range(6, -1, -1):
                em.op("vector",
                      lambda eng, suf=suf, eqc=eqc, j=j: eng.tensor_tensor(
                          out=suf[:, j:j + 1], in0=suf[:, j + 1:j + 2],
                          in1=eqc[:, j + 1:j + 2], op=ALU.mult),
                      reads=[("suf", tt, j + 1), ("eq", tt)],
                      writes=[("suf", tt, j)])
            em.op("vector",
                  lambda eng, sel=sel, suf=suf: eng.tensor_tensor(
                      out=WMTOK[:, 8:16], in0=sel, in1=suf, op=ALU.mult),
                  reads=[("sel", tt)] + [("suf", tt, j) for j in range(8)],
                  writes=["wmt"])
            em.op("vector",
                  lambda eng, sc_=sc_: eng.tensor_tensor(
                      out=WMTOK[:, 0:8], in0=sc_,
                      in1=WMTOK[:, 8:16].bitcast(BF16), op=ALU.mult),
                  reads=[("sc", tt), "wmt"], writes=["wmt"])
            em.op("tensor",
                  lambda eng, tt=tt: eng.transpose(
                      out=PP[6][:, 0:64].bitcast(BF16)[0:16, 0:128],
                      in_=WMTOK[:], identity=IDENT[:]),
                  reads=["wmt", "ident"], writes=[("pp", 6)])
            em.op("scalar",
                  lambda eng, tt=tt: eng.copy(
                      out=WM16[:, tt * 128:(tt + 1) * 128],
                      in_=PP[6][:, 0:64].bitcast(BF16)[0:16, 0:128]),
                  reads=[("pp", 6)], writes=["wm16"])
        for i in range(N_EXP):
            def mm_wrep(eng, i=i):
                ins = None
                for g in range(TH // 128):
                    ins = eng.matmul(out=PP[4][:, g * 128:(g + 1) * 128],
                                     lhsT=RE16[:, i * 128:(i + 1) * 128],
                                     rhs=WM16[:, g * 128:(g + 1) * 128],
                                     start=True, stop=True)
                return ins
            em.op("tensor", mm_wrep, reads=["re16", "wm16"], writes=[("pp", 4)])
            em.op("scalar",
                  lambda eng: eng.copy(out=WREP[:], in_=PP[4][:, 0:TH]),
                  reads=[("pp", 4)], writes=["wrep"])
            def mm_mrep(eng, i=i):
                ins = None
                for g in range(TH // 128):
                    ins = eng.matmul(out=PP[5][:, g * 128:(g + 1) * 128],
                                     lhsT=RE16[:, (8 + i) * 128:(9 + i) * 128],
                                     rhs=WM16[:, g * 128:(g + 1) * 128],
                                     start=True, stop=True)
                return ins
            em.op("tensor", mm_mrep, reads=["re16", "wm16"], writes=[("pp", 5)])
            em.op("scalar",
                  lambda eng: eng.copy(out=MREP[:], in_=PP[5][:, 0:TH]),
                  reads=[("pp", 5)], writes=["mrep"])
            em.op("scalar",
                  lambda eng: eng.copy(out=MROW[:], in_=MREP[0:1, :]),
                  reads=["mrep"], writes=["mrow"])
            for k in range(8):
                em.op("vector",
                      lambda eng, k=k: eng.tensor_tensor(
                          out=XIB[:, k * TH:(k + 1) * TH],
                          in0=R[:, k * TH:(k + 1) * TH].bitcast(F32),
                          in1=WREP[:], op=ALU.mult),
                      reads=["R", "wrep"], writes=[("xib", k)])
            em.dma(B1S[:], b1r[i], writes=["b1s"], queue="gpsimd")
            for hc in range(8):
                sl = hc % 3
                em.dma(W1S[sl][:].rearrange("p (k c) -> p k c", k=8),
                       w1T4[i, :, :, hc * 512:(hc + 1) * 512].rearrange("k p c -> p k c"),
                       writes=[("w1s", sl)], queue="gpsimd", stream=f"w1_{sl}")
                for hs in range(4):
                    pidx = 4 + (hs % 2)
                    def mm_w1b(eng, sl=sl, hs=hs, hc=hc, pidx=pidx):
                        ins = None
                        for k in range(8):
                            ins = eng.matmul(
                                out=PP[pidx][:, 0:TH],
                                lhsT=W1S[sl][:, k * 512 + hs * 128: k * 512 + hs * 128 + 128],
                                rhs=XIB[:, k * TH:(k + 1) * TH],
                                start=(k == 0), stop=False)
                        c0 = hc * 512 + hs * 128
                        ins = eng.matmul(out=PP[pidx][:, 0:TH],
                                         lhsT=B1S[0:1, c0:c0 + 128],
                                         rhs=MROW[:], start=False, stop=True)
                        return ins
                    em.op("tensor", mm_w1b,
                          reads=[("w1s", sl), "b1s", "mrow"] +
                                [("xib", k) for k in range(8)],
                          writes=[("pp", pidx)])
                    em.op("scalar",
                          lambda eng, hc=hc, hs=hs, pidx=pidx: eng.activation(
                              out=HBUF[:, (hc * 4 + hs) * TH:(hc * 4 + hs + 1) * TH],
                              in_=PP[pidx][:, 0:TH], func=AF.Relu),
                          reads=[("pp", pidx)], writes=[("hb", hc * 4 + hs)])
            for ksg in range(8):
                sl = ksg % 3
                em.dma(W2S[sl][:].rearrange("p (k c) -> p k c", k=4),
                       w2T4[i, ksg * 4:(ksg + 1) * 4, :, :].rearrange("k p c -> p k c"),
                       writes=[("w2s", sl)], queue="gpsimd", stream=f"w2_{sl}")
                for oc in range(8):
                    def mm_w2(eng, sl=sl, ksg=ksg, oc=oc):
                        ins = None
                        for k in range(4):
                            ins = eng.matmul(
                                out=PP[oc][:, 0:TH],
                                lhsT=W2S[sl][:, k * 1024 + oc * 128: k * 1024 + oc * 128 + 128],
                                rhs=HBUF[:, (ksg * 4 + k) * TH:(ksg * 4 + k + 1) * TH],
                                start=(ksg == 0 and k == 0), stop=(ksg == 7 and k == 3))
                        return ins
                    em.op("tensor", mm_w2,
                          reads=[("w2s", sl)] +
                                [("hb", ksg * 4 + k) for k in range(4)],
                          writes=[("pp", oc)])
            for oc in range(8):
                em.op("vector",
                      lambda eng, oc=oc: eng.tensor_tensor(
                          out=H1[:, oc * TH:(oc + 1) * TH],
                          in0=H1[:, oc * TH:(oc + 1) * TH].bitcast(F32),
                          in1=PP[oc][:, 0:TH], op=ALU.add),
                      reads=[("pp", oc), "H1"], writes=["H1"])
        for oc in range(8):
            def mm_b2(eng, oc=oc):
                ins = None
                for g in range(TH // 128):
                    ins = eng.matmul(
                        out=PP[oc][:, g * 128:(g + 1) * 128],
                        lhsT=B2S[:, oc * 128:(oc + 1) * 128],
                        rhs=WM16[:, g * 128:(g + 1) * 128],
                        start=True, stop=True)
                return ins
            em.op("tensor", mm_b2, reads=["b2s", "wm16"], writes=[("pp", oc)])
            em.op("vector",
                  lambda eng, oc=oc: eng.tensor_tensor(
                      out=H1[:, oc * TH:(oc + 1) * TH],
                      in0=H1[:, oc * TH:(oc + 1) * TH].bitcast(F32),
                      in1=PP[oc][:, 0:TH], op=ALU.add),
                  reads=[("pp", oc), "H1"], writes=["H1"])
        emit_ln(H1, "H1", dst=R, dst_key="R", dst2=H1, dst2_key="H1")
        for t, d, key in ((RC, ropeC, "rc"), (RS, ropeS, "rs"),
                          (RCK, ropeCk, "rck"), (RSK, ropeSk, "rsk")):
            em.dma(t[:], d[:, t0:t0 + TH], writes=[key], queue="gpsimd",
                   stream="rope_l")

        def tok_write(dst_dram, fdim, c_ofs, keybase):
            for tt in range(4):
                em.op("tensor",
                      lambda eng, tt=tt: eng.transpose(
                          out=PP[5][:, 0:64].bitcast(BF16)[:, 0:128],
                          in_=CBF[:, tt * 128:(tt + 1) * 128],
                          identity=IDENT[:]),
                      reads=["cbf", "ident"], writes=[("pp", 5)])
                em.op("scalar",
                      lambda eng, tt=tt: eng.copy(
                          out=COLL[:, tt * 128:(tt + 1) * 128],
                          in_=PP[5][:, 0:64].bitcast(BF16)[:, 0:128]),
                      reads=[("pp", 5)], writes=[("coll", tt)])
            em.dma(dst_dram.rearrange("(t f) -> t f", f=fdim)
                   [t0:t0 + TH, c_ofs * 128:(c_ofs + 1) * 128]
                   .rearrange("(tt p) f -> p tt f", p=128),
                   COLL[:].rearrange("p (tt f) -> p tt f", tt=4),
                   reads=[("coll", tt) for tt in range(4)],
                   writes=[keybase], stream=keybase)

        for oc in range(10):
            sl = oc % 2
            em.dma(AWS[sl][:].rearrange("p (k c) -> p k c", k=8),
                   aw[:, :, oc * 128:(oc + 1) * 128].rearrange("k p c -> p k c"),
                   writes=[("aws", sl)], stream=f"aw_{sl}")
            def mm_aw(eng, sl=sl):
                ins = None
                for k in range(8):
                    ins = eng.matmul(out=PP[sl][:, 0:TH],
                                     lhsT=AWS[sl][:, k * 128:(k + 1) * 128],
                                     rhs=R[:, k * TH:(k + 1) * TH],
                                     start=(k == 0), stop=(k == 7))
                return ins
            em.op("tensor", mm_aw, reads=[("aws", sl), "R"], writes=[("pp", sl)])
            if oc < 8:
                em.op("scalar",
                      lambda eng, oc=oc, sl=sl: eng.activation(
                          out=QC[:], in_=PP[sl][:, 0:TH], func=AF.Identity,
                          bias=AWBC[:, oc:oc + 1]),
                      reads=[("pp", sl), "awbc"], writes=["qc"])
                em.op("tensor",
                      lambda eng: eng.matmul(out=PP[4][:, 0:TH], lhsT=PERMQ[:],
                                             rhs=QC[:], start=True, stop=True),
                      reads=["permq", "qc"], writes=[("pp", 4)])
                em.op("vector",
                      lambda eng: eng.tensor_tensor(out=TMPA, in0=QC[:].bitcast(F32),
                                                    in1=RC[:], op=ALU.mult),
                      reads=["qc", "rc"], writes=["tmpa"])
                em.op("vector",
                      lambda eng: eng.tensor_tensor(out=TMPB, in0=PP[4][:, 0:TH],
                                                    in1=RS[:], op=ALU.mult),
                      reads=[("pp", 4), "rs"], writes=["tmpb"])
                em.op("vector",
                      lambda eng: eng.tensor_tensor(out=CBF[:], in0=TMPA,
                                                    in1=TMPB, op=ALU.add),
                      reads=["tmpa", "tmpb"], writes=["cbf"])
                tok_write(qtok, 1024, oc, "qtok_w")
            else:
                em.op("scalar",
                      lambda eng, oc=oc, sl=sl: eng.activation(
                          out=LAT[:, (oc - 8) * TH:(oc - 7) * TH],
                          in_=PP[sl][:, 0:TH], func=AF.Identity,
                          bias=AWBC[:, oc:oc + 1]),
                      reads=[("pp", sl), "awbc"], writes=[("lat", oc - 8)])
        for g_ in range(16):
            sl = g_ % 2
            em.dma(AWS[sl][:, 0:2 * 128].rearrange("p (k c) -> p k c", k=2),
                   wupT[:, :, g_ * 128:(g_ + 1) * 128].rearrange("k p c -> p k c"),
                   writes=[("aws", sl)], stream=f"aw_{sl}")
            def mm_up(eng, sl=sl):
                ins = None
                for k in range(2):
                    ins = eng.matmul(out=PP[sl][:, 0:TH],
                                     lhsT=AWS[sl][:, k * 128:(k + 1) * 128],
                                     rhs=LAT[:, k * TH:(k + 1) * TH],
                                     start=(k == 0), stop=(k == 1))
                return ins
            em.op("tensor", mm_up,
                  reads=[("aws", sl), ("lat", 0), ("lat", 1)],
                  writes=[("pp", sl)])
            em.op("scalar",
                  lambda eng, g_=g_, sl=sl: eng.activation(
                      out=QC[:], in_=PP[sl][:, 0:TH], func=AF.Identity,
                      bias=BUPC[:, g_:g_ + 1]),
                  reads=[("pp", sl), "bupc"], writes=["qc"])
            em.op("tensor",
                  lambda eng: eng.matmul(out=PP[4][:, 0:TH], lhsT=PERMKV[:],
                                         rhs=QC[:], start=True, stop=True),
                  reads=["permkv", "qc"], writes=[("pp", 4)])
            em.op("vector",
                  lambda eng: eng.tensor_tensor(out=TMPA, in0=QC[:].bitcast(F32),
                                                in1=RCK[:], op=ALU.mult),
                  reads=["qc", "rck"], writes=["tmpa"])
            em.op("vector",
                  lambda eng: eng.tensor_tensor(out=TMPB, in0=PP[4][:, 0:TH],
                                                in1=RSK[:], op=ALU.mult),
                  reads=[("pp", 4), "rsk"], writes=["tmpb"])
            em.op("vector",
                  lambda eng: eng.tensor_tensor(out=CBF[:], in0=TMPA,
                                                in1=TMPB, op=ALU.add),
                  reads=["tmpa", "tmpb"], writes=["cbf"])
            tok_write(kvtok, 2048, g_, "kvtok_w")
        pend = None
        rounds = list(range(TH // 8))
        for r_ in rounds:
            ti = t0 + r_ * 8
            sl = r_ % 2
            em.dma(RKV[sl][:],
                   kvtok.rearrange("(r p f) -> r p f", p=128, f=128)[ti // 8],
                   reads=["kvtok_w"], writes=[("rkv", sl)], stream=f"rkv_{sl}")
            em.dma(RQL[sl][:],
                   qtok.rearrange("(r p f) -> r p f", p=128, f=64)[ti // 8],
                   reads=["qtok_w"], writes=[("rql", sl)], stream=f"rql_{sl}")
            em.op("tensor",
                  lambda eng, sl=sl: eng.transpose(
                      out=PP[6][:, 0:64].bitcast(BF16)[0:64, 0:128],
                      in_=RKV[sl][:, 0:64], identity=IDENT[:]),
                  reads=[("rkv", sl), "ident"], writes=[("pp", 6)])
            em.op("scalar",
                  lambda eng, sl=sl: eng.copy(out=RKT[sl][0:64, :],
                                              in_=PP[6][:, 0:64].bitcast(BF16)[0:64, 0:128]),
                  reads=[("pp", 6)], writes=[("rktd", sl)])
            em.op("tensor",
                  lambda eng, sl=sl: eng.transpose(
                      out=PP[7][:, 0:64].bitcast(BF16)[0:64, 0:128],
                      in_=RQL[sl][:], identity=IDENT[:]),
                  reads=[("rql", sl), "ident"], writes=[("pp", 7)])
            em.op("scalar",
                  lambda eng, sl=sl: eng.copy(out=RQT[sl][0:64, :],
                                              in_=PP[7][:, 0:64].bitcast(BF16)[0:64, 0:128]),
                  reads=[("pp", 7)], writes=[("rqtd", sl)])
            em.op("tensor",
                  lambda eng, sl=sl: eng.matmul(out=PP[2][:, sl * 128:sl * 128 + 128],
                                                lhsT=RKT[sl][:], rhs=RQT[sl][:],
                                                start=True, stop=True),
                  reads=[("rktd", sl), ("rqtd", sl), ("augq", sl), ("augk", sl)],
                  writes=[("psc", sl)])
            em.op("scalar",
                  lambda eng, sl=sl: eng.activation(out=RP[sl][:],
                                                    in_=PP[2][:, sl * 128:sl * 128 + 128],
                                                    func=AF.Exp, bias=NEG20[:, 0:1]),
                  reads=[("psc", sl), "neg20"], writes=[("rp", sl)])
            def emit_pv(rr):
                sl2 = rr % 2
                ti2 = t0 + rr * 8
                em.op("tensor",
                      lambda eng, sl2=sl2: eng.matmul(
                          out=PP[3][:, sl2 * 128:sl2 * 128 + 65], lhsT=RP[sl2][:],
                          rhs=RKV[sl2][:, 64:128], start=True, stop=False),
                      reads=[("rp", sl2), ("rkv", sl2)], writes=[("ppv", sl2)]) \
                    if False else None
                def mm_pv(eng, sl2=sl2):
                    eng.matmul(out=PP[3][:, sl2 * 128:sl2 * 128 + 64],
                               lhsT=RP[sl2][:], rhs=RKV[sl2][:, 64:128],
                               start=True, stop=True)
                    return eng.matmul(out=PP[3][:, sl2 * 128 + 64:sl2 * 128 + 68],
                                      lhsT=RP[sl2][:], rhs=RV[:, 64:68],
                                      start=True, stop=True)
                em.op("tensor", mm_pv,
                      reads=[("rp", sl2), ("rkv", sl2), "vones"],
                      writes=[("ppv", sl2)])
                em.op("vector",
                      lambda eng, sl2=sl2: eng.reciprocal(
                          out=RREC[:, 4:5], in_=PP[3][:, sl2 * 128 + 64:sl2 * 128 + 65]),
                      reads=[("ppv", sl2)], writes=["rrec"])
                em.op("vector",
                      lambda eng, sl2=sl2: eng.tensor_scalar_mul(
                          out=RO[sl2][:], in0=PP[3][:, sl2 * 128:sl2 * 128 + 64],
                          scalar1=RREC[:, 4:5]),
                      reads=[("ppv", sl2), "rrec"], writes=[("ro", sl2)])
                em.dma(atok.rearrange("(r p f) -> r p f", p=128, f=64)[ti2 * 16 // 128],
                       RO[sl2][:], reads=[("ro", sl2)], writes=["atok_w"],
                       stream="atok_w")
            if pend is not None:
                emit_pv(pend)
            pend = r_
        emit_pv(pend)
        for c in range(8):
            for tt in range(4):
                sl2 = (c * 4 + tt) % 2
                em.dma(ATL[sl2][:],
                       atok.rearrange("(t f) -> t f", f=1024)
                       [t0 + tt * 128:t0 + (tt + 1) * 128, c * 128:(c + 1) * 128],
                       reads=["atok_w"], writes=[("atl", sl2)],
                       stream=f"atl_{sl2}")
                em.op("tensor",
                      lambda eng, sl2=sl2: eng.transpose(
                          out=PP[5][:, 0:64].bitcast(BF16)[:, 0:128],
                          in_=ATL[sl2][:], identity=IDENT[:]),
                      reads=[("atl", sl2), "ident"], writes=[("pp", 5)])
                em.op("scalar",
                      lambda eng, c=c, tt=tt: eng.copy(
                          out=XIB[:, c * TH + tt * 128: c * TH + (tt + 1) * 128],
                          in_=PP[5][:, 0:64].bitcast(BF16)[:, 0:128]),
                      reads=[("pp", 5)], writes=[("xib", c)])
        for oc in range(8):
            sl = oc % 2
            em.dma(W1S[sl][:, 0:8 * 128].rearrange("p (k c) -> p k c", k=8),
                   woutT[:, :, oc * 128:(oc + 1) * 128].rearrange("k p c -> p k c"),
                   writes=[("w1s", sl)], queue="gpsimd", stream=f"w1_{sl}")
            def mm_wout(eng, sl=sl):
                ins = None
                for k in range(8):
                    ins = eng.matmul(out=PP[sl][:, 0:TH],
                                     lhsT=W1S[sl][:, k * 128:(k + 1) * 128],
                                     rhs=XIB[:, k * TH:(k + 1) * TH],
                                     start=(k == 0), stop=(k == 7))
                return ins
            em.op("tensor", mm_wout,
                  reads=[("w1s", sl)] + [("xib", k) for k in range(8)],
                  writes=[("pp", sl)])
            em.op("vector",
                  lambda eng, oc=oc, sl=sl: eng.scalar_tensor_tensor(
                      out=H1[:, oc * TH:(oc + 1) * TH],
                      in0=PP[sl][:, 0:TH], scalar=BOUTC[:, oc:oc + 1],
                      in1=H1[:, oc * TH:(oc + 1) * TH].bitcast(F32),
                      op0=ALU.add, op1=ALU.add),
                  reads=[("pp", sl), "boutc", "H1"], writes=["H1"])
        emit_ln(H1, "H1", dst=R, dst_key="R")
        em.dma(a_o.rearrange("(k p) t -> p k t", p=128)[:, :, t0:t0 + TH],
               R[:].bitcast(F32).rearrange("p (k t) -> p k t", k=8),
               reads=["R"], writes=["a_w"], stream="a_w")
        emit_ln(R, "R", g_col=LN1GB[:, 0:8], b_col=LN1GB[:, 8:16],
                gb_key="ln1gb", dst=H1, dst_key="H1")
        for oc in range(24):
            sl = oc % 2
            em.dma(AWS[sl][:].rearrange("p (k c) -> p k c", k=8),
                   wqkv[:, :, oc * 128:(oc + 1) * 128].rearrange("k p c -> p k c"),
                   writes=[("aws", sl)], stream=f"aw_{sl}")
            def mm_qkv(eng, sl=sl):
                ins = None
                for k in range(8):
                    ins = eng.matmul(out=PP[sl][:, 0:TH],
                                     lhsT=AWS[sl][:, k * 128:(k + 1) * 128],
                                     rhs=H1[:, k * TH:(k + 1) * TH],
                                     start=(k == 0), stop=(k == 7))
                return ins
            em.op("tensor", mm_qkv, reads=[("aws", sl), "H1"],
                  writes=[("pp", sl)])
            em.op("scalar",
                  lambda eng, oc=oc, sl=sl: eng.activation(
                      out=QC[:], in_=PP[sl][:, 0:TH], func=AF.Identity,
                      bias=BQKVC[:, oc:oc + 1]),
                  reads=[("pp", sl), "bqkvc"], writes=["qc"])
            dst = [q_o, k_o, v_o][oc // 8]
            cc = oc % 8
            em.dma(dst.rearrange("(k p) t -> p k t", p=128)[:, cc:cc + 1, t0:t0 + TH],
                   QC[:].bitcast(F32).unsqueeze(1),
                   reads=["qc"], writes=["qkv_w"], stream="qkv_w")
    em.final_wait("gpsimd")
    return em.realize()



def prep_B_inputs(inputs, po_by_batch, w_bf16):
    w1T4 = w_bf16["w1T4"]
    w2T4 = w_bf16["w2T4"]
    woutT_bf = w_bf16["woutT"]
    def cols(c):
        b, q = divmod(c, 4)
        return b, list(range(q * 1024, (q + 1) * 1024)) + \
            list(range((7 - q) * 1024, (8 - q) * 1024))
    x = np.asarray(inputs["L0_attn_x"])
    gw = np.asarray(inputs["L0_ffn_gate_w"])
    gb = np.asarray(inputs["L0_ffn_gate_b"])
    w1 = np.asarray(inputs["L0_ffn_w1"])
    b1 = np.asarray(inputs["L0_ffn_b1"])
    w2 = np.asarray(inputs["L0_ffn_w2"])
    b2 = np.asarray(inputs["L0_ffn_b2"])
    Wq = np.asarray(inputs["L1_attn_Wq"]); bq = np.asarray(inputs["L1_attn_bq"])
    Wd = np.asarray(inputs["L1_attn_Wdown"]); bd = np.asarray(inputs["L1_attn_bdown"])
    Wu = np.asarray(inputs["L1_attn_Wup"]); bu = np.asarray(inputs["L1_attn_bup"])
    Wo = np.asarray(inputs["L1_attn_Wout"]); bo = np.asarray(inputs["L1_attn_bout"])
    l1g = np.asarray(inputs["L1_ffn_ln1_g"]); l1b = np.asarray(inputs["L1_ffn_ln1_b"])
    Wqf = np.asarray(inputs["L1_ffn_Wq"]); bqf = np.asarray(inputs["L1_ffn_bq"])
    Wkf = np.asarray(inputs["L1_ffn_Wk"]); bkf = np.asarray(inputs["L1_ffn_bk"])
    Wvf = np.asarray(inputs["L1_ffn_Wv"]); bvf = np.asarray(inputs["L1_ffn_bv"])

    gwT = f32(gw.T.reshape(8, 128, 8).transpose(1, 0, 2).reshape(128, 64))
    repE16 = np.zeros((16, 16 * 128), np.float32)
    for i in range(16):
        repE16[i, i * 128:(i + 1) * 128] = 1.0
    b2pad = np.zeros((16, 1024), np.float32)
    b2pad[8:16] = b2
    awf = f32(np.concatenate([Wq, Wd], axis=1).reshape(8, 128, 1280))
    awbc = f32(np.concatenate([bq.reshape(8, 128).T, bd.reshape(2, 128).T], axis=1))
    wupT = f32(Wu.reshape(2, 128, 2048))
    bupc = f32(bu.reshape(16, 128).T)
    boutc = f32(bo.reshape(8, 128).T)
    permq = np.zeros((128, 128), np.float32)
    permkv = np.zeros((128, 128), np.float32)
    for p in range(128):
        blk, d = divmod(p, 64)
        partner = blk * 64 + (d + 32 if d < 32 else d - 32)
        permq[partner, p] = 1.0
        if blk == 0:
            permkv[partner, p] = 1.0
        else:
            permkv[p, p] = 1.0
    augk = np.zeros((9, 128), np.float32)
    augq = np.zeros((9, 128), np.float32)
    for j in range(8):
        for p in range(128):
            if p // 16 == j:
                augk[j, p] = 1.0
            else:
                augq[j, p] = NEG
    ident = np.eye(128, dtype=np.float32)
    ln1gbd = f32(np.concatenate([l1g.reshape(8, 128).T, l1b.reshape(8, 128).T], axis=1))
    wqkv = f32(np.concatenate([Wqf, Wkf, Wvf], axis=1).reshape(8, 128, 3072))
    bqkvc = f32(np.concatenate([bqf.reshape(8, 128).T, bkf.reshape(8, 128).T,
                                bvf.reshape(8, 128).T], axis=1))
    inv_freq = 1.0 / 10000.0 ** (np.arange(0, 64, 2, dtype=np.float64) / 64.0)

    maps = []
    for c in range(N_CORES):
        b, colix = cols(c)
        pos = np.array(colix, dtype=np.float64)
        fr = pos[None, :] * inv_freq[:, None]
        sin = np.sin(fr); cos = np.cos(fr)
        rC = np.empty((128, 2048), np.float32)
        rS = np.empty((128, 2048), np.float32)
        for r in range(128):
            d = r % 64
            j = d % 32
            rC[r] = cos[j]
            rS[r] = -sin[j] if d < 32 else sin[j]
        rCk = rC.copy(); rSk = rS.copy()
        rCk[64:, :] = 1.0
        rSk[64:, :] = 0.0
        rC = rC * 0.125
        rS = rS * 0.125
        xTs = f32(x[b].T[:, colix])
        po4 = f32(np.stack([po_by_batch[b][k][:, colix] for k in range(4)]))
        maps.append({
            "xT": xTs, "po4": po4, "gwT": gwT, "gbc": f32(gb.reshape(8, 1)),
            "w1T4": w1T4, "b1r": f32(b1.reshape(8, 1, 4096)), "w2T4": w2T4,
            "b2sd": b2pad, "repE16d": repE16,
            "onescol": np.ones((128, 1), np.float32),
            "ones1": np.ones((1, 128), np.float32),
            "aw": awf, "awbc": awbc, "wupT": wupT, "bupc": bupc,
            "woutT": woutT_bf, "boutc": boutc,
            "ropeC": rC, "ropeS": rS, "ropeCk": rCk, "ropeSk": rSk,
            "permq": f32(permq), "permkv": f32(permkv),
            "augkd": augk, "augqd": augq, "identd": ident,
            "ln1gbd": ln1gbd, "wqkv": wqkv, "bqkvc": bqkvc,
        })
    return maps



def build_C1():
    em = Emitter("C1")
    S_ = SEQ
    NQB = S_ // 512
    kT2 = em.dram("kT2", [2, 128, S_], F32R)
    qT2 = em.dram("qT2", [2, 128, S_], F32R)
    vtile = em.dram("vtile", [2, 64, 2, 128, 65], F32)
    dmask = em.dram("dmask", [128, 4 * 512], F32)
    ones1 = em.dram("ones1", [1, 128], F32R)
    scl = em.dram("scl", [128, 2], F32)
    ao = em.dram("attout", [256, S_], F32, kind="ExternalOutput")

    KC = [em.sbuf(f"KC{i}", [128, S_], F32R) for i in range(2)]
    QC_ = [em.sbuf(f"QC{i}", [128, S_], F32R) for i in range(2)]
    VC = [em.sbuf(f"VC{i}", [128, 64 * 2 * 65], BF16) for i in range(2)]
    DM = em.sbuf("DM", [128, 4 * 512], BF16)
    ONES1 = em.sbuf("ONES1", [1, 128], F32R)
    SCL = em.sbuf("SCL", [128, 2], F32)
    PT = [em.sbuf(f"PT{i}", [128, 512], BF16) for i in range(2)]
    SUMS = em.sbuf("SUMS", [1, 512], F32R)
    SUMSF = em.sbuf("SUMSF", [1, 512], F32)
    OEV = em.sbuf("OEV", [64, 512], F32R)
    AOUT = em.sbuf("AOUT", [64, 512], F32R)
    PP = [em.psum(f"pp{i}", [128, 512]) for i in range(4)]

    em.dma(DM[:], dmask, writes=["dm"], queue="gpsimd")
    em.dma(ONES1[:], ones1, writes=["ones1"])
    em.dma(SCL[:], scl, writes=["scl"])

    for pi in range(2):
        em.dma(KC[pi][:], kT2[pi], writes=[("kc", pi)])
        em.dma(QC_[pi][:], qT2[pi], writes=[("qc", pi)])
        for kbs in range(8):
            em.dma(VC[pi][:, kbs * 8 * 130:(kbs + 1) * 8 * 130]
                   .rearrange("p (a b c) -> p a b c", a=8, b=2),
                   vtile[pi, kbs * 8:(kbs + 1) * 8].rearrange("a b p c -> p a b c"),
                   writes=[("vc", pi)], queue="gpsimd", stream=f"vc_{pi}")
    for pi in range(2):
        for hp in range(2):
            h64 = hp * 64
            for qb in range(NQB):
                nkb = 4 * (qb + 1)
                for kb in range(nkb):
                    psc = PP[kb % 2]
                    em.op("tensor",
                          lambda eng, pi=pi, h64=h64, qb=qb, kb=kb, psc=psc:
                          eng.matmul(
                              out=psc[:, 0:512],
                              lhsT=KC[pi][h64:h64 + 64, kb * 128:(kb + 1) * 128],
                              rhs=QC_[pi][h64:h64 + 64, qb * 512:(qb + 1) * 512],
                              start=True, stop=True),
                          reads=[("kc", pi), ("qc", pi)],
                          writes=[("pp", kb % 2)])
                    pts = PT[kb % 2]
                    em.op("scalar",
                          lambda eng, psc=psc, pts=pts: eng.activation(
                              out=pts[:], in_=psc[:, 0:512], func=AF.Exp,
                              scale=SCL[:, 0:1], bias=SCL[:, 1:2]),
                          reads=[("pp", kb % 2), "scl"], writes=[("pt", kb % 2)])
                    j = kb - (nkb - 4)
                    if j >= 0:
                        em.op("vector",
                              lambda eng, j=j, pts=pts: eng.tensor_tensor(
                                  out=pts[:], in0=pts[:],
                                  in1=DM[:, j * 512:(j + 1) * 512], op=ALU.mult),
                              reads=[("pt", kb % 2), "dm"], writes=[("pt", kb % 2)])
                    em.op("tensor",
                          lambda eng, pi=pi, hp=hp, kb=kb, pts=pts: eng.matmul(
                              out=PP[2][0:65, 0:512],
                              lhsT=VC[pi][:, (kb * 2 + hp) * 65:(kb * 2 + hp + 1) * 65],
                              rhs=pts[:],
                              start=(kb == 0), stop=(kb == nkb - 1)),
                          reads=[("vc", pi), ("pt", kb % 2)], writes=[("pp", 2)])
                em.op("vector",
                      lambda eng: eng.reciprocal(out=SUMSF[:], in_=PP[2][64:65, 0:512]),
                      reads=[("pp", 2)], writes=["sumsf"])
                em.op("scalar",
                      lambda eng: eng.copy(out=SUMS[:], in_=SUMSF[:]),
                      reads=["sumsf"], writes=["sums"])
                em.op("tensor",
                      lambda eng: eng.matmul(out=PP[3][0:64, 0:512],
                                             lhsT=ONES1[0:1, 0:64], rhs=SUMS[:],
                                             start=True, stop=True),
                      reads=["ones1", "sums"], writes=[("pp", 3)])
                em.op("scalar",
                      lambda eng: eng.copy(out=OEV[:], in_=PP[2][0:64, 0:512]),
                      reads=[("pp", 2)], writes=["oev"])
                em.op("vector",
                      lambda eng: eng.tensor_tensor(
                          out=AOUT[:], in0=OEV[:].bitcast(F32),
                          in1=PP[3][0:64, 0:512], op=ALU.mult),
                      reads=["oev", ("pp", 3)], writes=["aout"])
                hh = pi * 2 + hp
                em.dma(ao[hh * 64:(hh + 1) * 64, 0:S_][:, qb * 512:(qb + 1) * 512],
                       AOUT[:].bitcast(F32),
                       reads=["aout"], writes=["ao_w"], stream="ao_w")
    em.final_wait("gpsimd")
    return em.realize()


def prep_C1_inputs(b_results):
    kf = [np.empty((1024, SEQ), np.float32) for _ in range(2)]
    qf = [np.empty((1024, SEQ), np.float32) for _ in range(2)]
    vf = [np.empty((1024, SEQ), np.float32) for _ in range(2)]
    for c in range(8):
        b, q = divmod(c, 4)
        colix = list(range(q * 1024, (q + 1) * 1024)) + \
            list(range((7 - q) * 1024, (8 - q) * 1024))
        kf[b][:, colix] = b_results[c]["kT"]
        qf[b][:, colix] = b_results[c]["qT"]
        vf[b][:, colix] = b_results[c]["vT"]
    dmask = np.zeros((128, 4 * 512), np.float32)
    for j in range(4):
        for k in range(128):
            for qq in range(512):
                if qq >= k + j * 128:
                    dmask[k, j * 512 + qq] = 1.0
    scl = np.zeros((128, 2), np.float32)
    scl[:, 0] = 0.125
    scl[:, 1] = -20.0
    maps = []
    for c in range(8):
        b, hg = divmod(c, 4)
        r0 = hg * 256
        kT2 = f32(kf[b][r0:r0 + 256].reshape(2, 128, SEQ))
        qT2 = f32(qf[b][r0:r0 + 256].reshape(2, 128, SEQ))
        vt = np.ones((2, 64, 2, 128, 65), np.float32)
        for pi in range(2):
            for hp in range(2):
                vh = vf[b][r0 + (pi * 2 + hp) * 64: r0 + (pi * 2 + hp + 1) * 64]
                vt[pi, :, hp, :, 0:64] = vh.T.reshape(64, 128, 64)
        maps.append({
            "kT2": kT2, "qT2": qT2, "vtile": f32(vt), "dmask": dmask,
            "ones1": np.ones((1, 128), np.float32), "scl": scl,
        })
    return maps



def build_C2():
    em = Emitter("C2")
    TH = 512
    NQ = 4
    TPC = TH * NQ
    NEGB = -30000.0
    aT = em.dram("aT", [D_MODEL, TPC], F32R)
    attT = em.dram("attT", [D_MODEL, TPC], F32R)
    woT = em.dram("woT", [8, 128, D_MODEL], F32R)
    boc = em.dram("boc", [128, 8], F32)
    ln2gbd = em.dram("ln2gbd", [128, 16], F32)
    wrT = em.dram("wrT", [128, 8 * N_EXP], F32)
    brc = em.dram("brc", [N_EXP, 1], F32)
    ew14 = em.dram("ew14", [N_EXP, 8, 128, D_FF], BF16)
    eb1c = em.dram("eb1c", [128, N_EXP * 32], F32)
    ew24 = em.dram("ew24", [N_EXP, 32, 128, D_MODEL], BF16)
    eb2s = em.dram("eb2s", [16, D_MODEL], F32)
    repE16d = em.dram("repE16d", [16, 16 * 128], F32)
    onescol = em.dram("onescol", [128, 1], F32R)
    ones1 = em.dram("ones1", [1, 128], F32R)
    identd = em.dram("identd", [128, 128], F32)
    identr = em.dram("identr", [128, 128], F32R)
    out_o = em.dram("out_tok", [TPC, D_MODEL], F32, kind="ExternalOutput")
    wscr = em.dram("wscr", [8, 4, 128], BF16, kind="Internal")

    R = em.sbuf("R", [128, 8 * TH], F32R)
    ACC = em.sbuf("ACC", [128, 8 * TH], F32R)
    H2 = em.sbuf("H2", [128, 8 * TH], F32R)
    H2B = em.sbuf("H2B", [128, 8 * TH], BF16)
    ATT = em.sbuf("ATT", [128, 8 * TH], F32R)
    HBUF = em.sbuf("HBUF", [128, 32 * TH], BF16)
    W1S = [em.sbuf(f"W1S{i}", [128, 8 * TH], BF16) for i in range(3)]
    W2S = [em.sbuf(f"W2S{i}", [128, 4 * 1024], BF16) for i in range(3)]
    WOS = [em.sbuf(f"WOS{i}", [128, 8 * 128], F32R) for i in range(2)]
    PS_ = em.sbuf("PS_", [128, 2 * TH], F32)
    REPS = em.sbuf("REPS", [128, 2 * TH], F32)
    SQT = em.sbuf("SQT", [128, TH], F32R)
    ROWS = em.sbuf("ROWS", [1, 3 * TH], F32R)
    ONESC = em.sbuf("ONESC", [128, 1], F32R)
    ONES1 = em.sbuf("ONES1", [1, 128], F32R)
    WR = em.sbuf("WR", [128, 8 * N_EXP], F32)
    BRC = em.sbuf("BRC", [N_EXP, 1], F32)
    EB1C = em.sbuf("EB1C", [128, N_EXP * 32], F32)
    EB2S = em.sbuf("EB2S", [16, D_MODEL], BF16)
    RE16 = em.sbuf("RE16", [16, 16 * 128], BF16)
    WMW = em.sbuf("WMW", [8, TH], BF16)
    WREP = em.sbuf("WREP", [128, TH], BF16)
    BOC = em.sbuf("BOC", [128, 8], F32)
    LN2GB = em.sbuf("LN2GB", [128, 16], F32)
    LGT = em.sbuf("LGT", [8, TH], F32)
    LTOK = em.sbuf("LTOK", [128, 4 * 48], F32)
    WMTOK = em.sbuf("WMTOK", [128, 16], BF16)
    IDF = em.sbuf("IDF", [128, 128], F32)
    IDR = em.sbuf("IDR", [128, 128], F32R)
    IDENTB = em.sbuf("IDENTB", [128, 128], BF16)
    WM16 = em.sbuf("WM16", [16, 4 * 128], BF16)
    COLL = em.sbuf("COLL", [128, 4 * 128], F32)
    RREC = em.sbuf("RREC", [128, 5], F32)
    TMPE = em.sbuf("TMPE", [128, TH], F32)
    PP = [em.psum(f"pp{i}", [128, 512]) for i in range(8)]
    TMPA = PS_[:, 0:TH]
    TMPB = PS_[:, TH:2 * TH]
    stat_m = ROWS[0:1, 0:TH]
    stat_r = ROWS[0:1, TH:2 * TH]
    stat_t = ROWS[0:1, 2 * TH:3 * TH]

    for t, d, key in ((WR, wrT, "wr"), (BRC, brc, "brc"), (ONESC, onescol, "onesc"),
                      (ONES1, ones1, "ones1"), (BOC, boc, "boc"),
                      (LN2GB, ln2gbd, "ln2gb"), (EB1C, eb1c, "eb1c"),
                      (IDF, identd, "idf"), (IDR, identr, "idr")):
        em.dma(t[:], d, writes=[key])
    for t, d, key in ((EB2S, eb2s, "eb2s"), (RE16, repE16d, "re16"),
                      (IDENTB, identd, "identb")):
        em.dma(t[:], d, writes=[key], queue="gpsimd")

    emit_ln = make_emit_ln(em, PP, ONESC, ONES1, SQT, stat_m,
                           stat_r, stat_t, REPS, TMPA, TMPB, TH)

    for qi in range(NQ):
        t0 = qi * TH
        em.dma(R[:].rearrange("p (k t) -> p k t", k=8),
               aT.rearrange("(k p) t -> p k t", p=128)[:, :, t0:t0 + TH],
               writes=["R"])
        em.dma(ATT[:].rearrange("p (k t) -> p k t", k=8),
               attT.rearrange("(k p) t -> p k t", p=128)[:, :, t0:t0 + TH],
               writes=["ATT"])
        for oc in range(8):
            sl = oc % 2
            em.dma(WOS[sl][:].rearrange("p (k c) -> p k c", k=8),
                   woT[:, :, oc * 128:(oc + 1) * 128].rearrange("k p c -> p k c"),
                   writes=[("wos", sl)], stream=f"wo_{sl}")
            def mm_wo(eng, sl=sl):
                ins = None
                for k in range(8):
                    ins = eng.matmul(out=PP[sl][:, 0:TH],
                                     lhsT=WOS[sl][:, k * 128:(k + 1) * 128],
                                     rhs=ATT[:, k * TH:(k + 1) * TH],
                                     start=(k == 0), stop=(k == 7))
                return ins
            em.op("tensor", mm_wo, reads=[("wos", sl), "ATT"], writes=[("pp", sl)])
            em.op("vector",
                  lambda eng, oc=oc, sl=sl: eng.scalar_tensor_tensor(
                      out=ACC[:, oc * TH:(oc + 1) * TH],
                      in0=PP[sl][:, 0:TH], scalar=BOC[:, oc:oc + 1],
                      in1=R[:, oc * TH:(oc + 1) * TH].bitcast(F32),
                      op0=ALU.add, op1=ALU.add),
                  reads=[("pp", sl), "boc", "R"], writes=["ACCk"])
        emit_ln(ACC, "ACCk", g_col=LN2GB[:, 0:8], b_col=LN2GB[:, 8:16],
                gb_key="ln2gb", dst=H2, dst_key="H2")
        for k in range(8):
            em.op("scalar",
                  lambda eng, k=k: eng.copy(
                      out=H2B[:, k * TH:(k + 1) * TH],
                      in_=H2[:, k * TH:(k + 1) * TH].bitcast(F32)),
                  reads=["H2"], writes=["H2B"])
        def mm_r(eng):
            ins = None
            for k in range(8):
                ins = eng.matmul(out=PP[5][0:8, 0:TH],
                                 lhsT=WR[:, k * 8:(k + 1) * 8],
                                 rhs=H2[:, k * TH:(k + 1) * TH].bitcast(F32),
                                 start=(k == 0), stop=(k == 7))
            return ins
        em.op("tensor", mm_r, reads=["wr", "H2"], writes=[("pp", 5)])
        em.op("scalar",
              lambda eng: eng.activation(out=LGT[:], in_=PP[5][0:8, 0:TH],
                                         func=AF.Identity, bias=BRC[:, 0:1]),
              reads=[("pp", 5), "brc"], writes=["lgt"])
        for tt in range(4):
            em.op("tensor",
                  lambda eng, tt=tt: eng.transpose(
                      out=PP[4][0:128, 0:8], in_=LGT[0:8, tt * 128:(tt + 1) * 128],
                      identity=IDF[0:8, 0:8]),
                  reads=["lgt", "idf"], writes=[("pp", 4)])
            base = tt * 48
            lg = LTOK[:, base:base + 8]
            ex = LTOK[:, base + 8:base + 16]
            eqc = LTOK[:, base + 16:base + 24]
            msk = LTOK[:, base + 24:base + 32]
            sel = LTOK[:, base + 32:base + 40]
            es = LTOK[:, base + 40:base + 48]
            em.op("vector",
                  lambda eng, lg=lg: eng.tensor_copy(out=lg, in_=PP[4][0:128, 0:8]),
                  reads=[("pp", 4)], writes=[("lg", tt)])
            em.op("vector",
                  lambda eng, lg=lg: eng.reduce_max(
                      out=RREC[:, 0:1], in_=lg, axis=AX.X, negate=True),
                  reads=[("lg", tt)], writes=["nmx"])
            em.op("scalar",
                  lambda eng, lg=lg, ex=ex: eng.activation(
                      out=ex, in_=lg, func=AF.Exp, bias=RREC[:, 0:1]),
                  reads=[("lg", tt), "nmx"], writes=["eg"])
            em.op("vector",
                  lambda eng, lg=lg, eqc=eqc: eng.tensor_scalar(
                      out=eqc, in0=lg, scalar1=RREC[:, 0:1], scalar2=0.0,
                      op0=ALU.add, op1=ALU.is_ge),
                  reads=[("lg", tt), "nmx"], writes=["eq"])
            em.op("vector",
                  lambda eng, lg=lg, eqc=eqc, msk=msk: eng.scalar_tensor_tensor(
                      out=msk, in0=eqc, scalar=NEGB, in1=lg,
                      op0=ALU.mult, op1=ALU.add),
                  reads=["eq", ("lg", tt)], writes=["msk"])
            em.op("vector",
                  lambda eng, msk=msk: eng.reduce_max(
                      out=RREC[:, 3:4], in_=msk, axis=AX.X, negate=True),
                  reads=["msk"], writes=["nthr"])
            em.op("vector",
                  lambda eng, lg=lg, sel=sel: eng.tensor_scalar(
                      out=sel, in0=lg, scalar1=RREC[:, 3:4], scalar2=0.0,
                      op0=ALU.add, op1=ALU.is_ge),
                  reads=[("lg", tt), "nthr"], writes=["sel"])
            em.op("vector",
                  lambda eng, ex=ex, sel=sel, es=es: eng.tensor_tensor(
                      out=es, in0=ex, in1=sel, op=ALU.mult),
                  reads=["eg", "sel"], writes=["es"])
            em.op("vector",
                  lambda eng, es=es: eng.reduce_sum(
                      out=RREC[:, 1:2], in_=es, axis=AX.X),
                  reads=["es"], writes=["essum"])
            em.op("vector",
                  lambda eng: eng.reciprocal(out=RREC[:, 2:3], in_=RREC[:, 1:2]),
                  reads=["essum"], writes=["rs"])
            em.op("vector",
                  lambda eng, es=es: eng.tensor_scalar_mul(
                      out=WMTOK[:, 0:8], in0=es, scalar1=RREC[:, 2:3]),
                  reads=["es", "rs"], writes=["wmt"])
            em.op("tensor",
                  lambda eng, tt=tt: eng.transpose(
                      out=PP[6][:, 0:64].bitcast(BF16)[0:16, 0:128],
                      in_=WMTOK[:], identity=IDENTB[:]),
                  reads=["wmt", "identb"], writes=[("pp", 6)])
            em.op("scalar",
                  lambda eng, tt=tt: eng.copy(
                      out=WM16[:, tt * 128:(tt + 1) * 128],
                      in_=PP[6][:, 0:64].bitcast(BF16)[0:16, 0:128]),
                  reads=[("pp", 6)], writes=["wm16"])
        for k in range(8):
            em.op("vector",
                  lambda eng, k=k: eng.tensor_tensor(
                      out=ACC[:, k * TH:(k + 1) * TH],
                      in0=ACC[:, k * TH:(k + 1) * TH].bitcast(F32),
                      in1=R[:, k * TH:(k + 1) * TH].bitcast(F32), op=ALU.add),
                  reads=["ACCk", "R"], writes=["ACCk"])
        for i in range(N_EXP):
            for hc in range(8):
                sl = hc % 3
                em.dma(W1S[sl][:].rearrange("p (k c) -> p k c", k=8),
                       ew14[i, :, :, hc * 512:(hc + 1) * 512].rearrange("k p c -> p k c"),
                       writes=[("w1s", sl)], queue="gpsimd", stream=f"w1_{sl}")
                for hs in range(4):
                    pidx = 4 + (hs % 2)
                    def mm_w1(eng, sl=sl, hs=hs, pidx=pidx):
                        ins = None
                        for k in range(8):
                            ins = eng.matmul(
                                out=PP[pidx][:, 0:TH],
                                lhsT=W1S[sl][:, k * 512 + hs * 128: k * 512 + hs * 128 + 128],
                                rhs=H2B[:, k * TH:(k + 1) * TH],
                                start=(k == 0), stop=(k == 7))
                        return ins
                    em.op("tensor", mm_w1,
                          reads=[("w1s", sl), "H2B"], writes=[("pp", pidx)])
                    em.op("scalar",
                          lambda eng, i=i, hc=hc, hs=hs, pidx=pidx: eng.activation(
                              out=HBUF[:, (hc * 4 + hs) * TH:(hc * 4 + hs + 1) * TH],
                              in_=PP[pidx][:, 0:TH], func=AF.Gelu,
                              bias=EB1C[:, i * 32 + hc * 4 + hs: i * 32 + hc * 4 + hs + 1]),
                          reads=[("pp", pidx), "eb1c"], writes=[("hb", hc * 4 + hs)])
            def mm_wrep(eng, i=i):
                ins = None
                for g in range(TH // 128):
                    ins = eng.matmul(out=PP[3][:, g * 128:(g + 1) * 128],
                                     lhsT=RE16[:, i * 128:(i + 1) * 128],
                                     rhs=WM16[:, g * 128:(g + 1) * 128],
                                     start=True, stop=True)
                return ins
            em.op("tensor", mm_wrep, reads=["re16", "wm16"], writes=[("pp", 3)])
            em.op("scalar",
                  lambda eng: eng.copy(out=WREP[:], in_=PP[3][:, 0:TH]),
                  reads=[("pp", 3)], writes=["wrep"])
            for ksg in range(8):
                sl = ksg % 3
                em.dma(W2S[sl][:].rearrange("p (k c) -> p k c", k=4),
                       ew24[i, ksg * 4:(ksg + 1) * 4, :, :].rearrange("k p c -> p k c"),
                       writes=[("w2s", sl)], queue="gpsimd", stream=f"w2_{sl}")
                for oc in range(8):
                    def mm_w2(eng, sl=sl, ksg=ksg, oc=oc):
                        ins = None
                        for k in range(4):
                            ins = eng.matmul(
                                out=PP[oc % 2][:, 0:TH] if False else PP[oc][:, 0:TH],
                                lhsT=W2S[sl][:, k * 1024 + oc * 128: k * 1024 + oc * 128 + 128],
                                rhs=HBUF[:, (ksg * 4 + k) * TH:(ksg * 4 + k + 1) * TH],
                                start=(ksg == 0 and k == 0), stop=(ksg == 7 and k == 3))
                        return ins
                    em.op("tensor", mm_w2,
                          reads=[("w2s", sl)] +
                                [("hb", ksg * 4 + k) for k in range(4)],
                          writes=[("pp", oc)])
            for oc in range(8):
                em.op("vector",
                      lambda eng, oc=oc: eng.tensor_tensor(
                          out=TMPE[:], in0=PP[oc][:, 0:TH], in1=WREP[:],
                          op=ALU.mult),
                      reads=[("pp", oc), "wrep"], writes=["tmpe"])
                em.op("vector",
                      lambda eng, oc=oc: eng.tensor_tensor(
                          out=ACC[:, oc * TH:(oc + 1) * TH],
                          in0=ACC[:, oc * TH:(oc + 1) * TH].bitcast(F32),
                          in1=TMPE[:], op=ALU.add),
                      reads=["tmpe", "ACCk"], writes=["ACCk"])
        for oc in range(8):
            def mm_b2(eng, oc=oc):
                ins = None
                for g in range(TH // 128):
                    ins = eng.matmul(
                        out=PP[oc][:, g * 128:(g + 1) * 128],
                        lhsT=EB2S[:, oc * 128:(oc + 1) * 128],
                        rhs=WM16[:, g * 128:(g + 1) * 128],
                        start=True, stop=True)
                return ins
            em.op("tensor", mm_b2, reads=["eb2s", "wm16"], writes=[("pp", oc)])
            em.op("vector",
                  lambda eng, oc=oc: eng.tensor_tensor(
                      out=ACC[:, oc * TH:(oc + 1) * TH],
                      in0=ACC[:, oc * TH:(oc + 1) * TH].bitcast(F32),
                      in1=PP[oc][:, 0:TH], op=ALU.add),
                  reads=[("pp", oc), "ACCk"], writes=["ACCk"])
        emit_ln(ACC, "ACCk", dst=H2, dst_key="H2")
        for c in range(8):
            for tt in range(4):
                em.op("tensor",
                      lambda eng, c=c, tt=tt: eng.transpose(
                          out=PP[5][:, 0:128].bitcast(F32R)[:, 0:128],
                          in_=H2[:, c * TH + tt * 128: c * TH + (tt + 1) * 128],
                          identity=IDR[:]),
                      reads=["H2", "idr"], writes=[("pp", 5)])
                em.op("scalar",
                      lambda eng, tt=tt: eng.copy(
                          out=COLL[:, tt * 128:(tt + 1) * 128],
                          in_=PP[5][:, 0:128]),
                      reads=[("pp", 5)], writes=[("coll", tt)])
            em.dma(out_o[t0:t0 + TH, c * 128:(c + 1) * 128]
                   .rearrange("(tt p) f -> p tt f", p=128),
                   COLL[:].rearrange("p (tt f) -> p tt f", tt=4),
                   reads=[("coll", tt) for tt in range(4)],
                   writes=["outw"], stream="outw")
    em.final_wait("gpsimd")
    return em.realize()


def prep_C2_inputs(inputs, b_results, c1_results, w_bf16):
    ew14 = w_bf16["ew14"]
    ew24 = w_bf16["ew24"]
    Wo = np.asarray(inputs["L1_ffn_Wo"]); bo = np.asarray(inputs["L1_ffn_bo"])
    l2g = np.asarray(inputs["L1_ffn_ln2_g"]); l2b = np.asarray(inputs["L1_ffn_ln2_b"])
    Wr = np.asarray(inputs["L1_ffn_Wr"]); br = np.asarray(inputs["L1_ffn_br"])
    ew1 = np.asarray(inputs["L1_ffn_e_w1"]); eb1 = np.asarray(inputs["L1_ffn_e_b1"])
    ew2 = np.asarray(inputs["L1_ffn_e_w2"]); eb2 = np.asarray(inputs["L1_ffn_e_b2"])
    attf = [np.empty((1024, SEQ), np.float32) for _ in range(2)]
    for b in range(2):
        for hg in range(4):
            attf[b][hg * 256:(hg + 1) * 256] = c1_results[b * 4 + hg]["attout"]
    woT = f32(Wo.reshape(8, 128, 1024))
    wrT = f32(Wr.reshape(8, 128, 8).transpose(1, 0, 2).reshape(128, 64))
    eb1c = f32(eb1.reshape(8, 32, 128).transpose(2, 0, 1).reshape(128, 256))
    repE16 = np.zeros((16, 16 * 128), np.float32)
    for i in range(16):
        repE16[i, i * 128:(i + 1) * 128] = 1.0
    eb2pad = np.zeros((16, 1024), np.float32)
    eb2pad[0:8] = eb2
    maps = []
    for c in range(N_CORES):
        b, q = divmod(c, 4)
        colix = list(range(q * 1024, (q + 1) * 1024)) + \
            list(range((7 - q) * 1024, (8 - q) * 1024))
        maps.append({
            "aT": b_results[c]["aT"],
            "attT": f32(attf[b][:, colix]),
            "woT": woT, "boc": f32(bo.reshape(8, 128).T),
            "ln2gbd": f32(np.concatenate([l2g.reshape(8, 128).T,
                                          l2b.reshape(8, 128).T], axis=1)),
            "wrT": wrT, "brc": f32(br.reshape(8, 1)),
            "ew14": ew14, "eb1c": eb1c, "ew24": ew24, "eb2s": eb2pad,
            "repE16d": repE16,
            "onescol": np.ones((128, 1), np.float32),
            "ones1": np.ones((1, 128), np.float32),
            "identd": np.eye(128, dtype=np.float32),
            "identr": np.eye(128, dtype=np.float32),
        })
    return maps



def kernel(**inputs):
    maps_A = prep_A_inputs(inputs)
    ncA = build_A()
    resA = run_bass_kernel_spmd(ncA, maps_A, core_ids=list(range(N_CORES)))
    a_results = [resA.results[c] for c in range(N_CORES)]

    ncS = build_S()
    resS = run_bass_kernel_spmd(ncS, prep_S_inputs(inputs, a_results),
                                core_ids=list(range(N_CORES)))
    po_by_batch = [[resS.results[b * 4 + k]["po"] for k in range(4)]
                   for b in range(2)]
    wbfA = np.concatenate([a_results[c]["wbf"].reshape(-1) for c in range(8)])
    w1n = 8 * 8 * 128 * 4096
    wbfS = np.concatenate([resS.results[c]["wbf"].reshape(-1) for c in range(8)])
    e1n = 8 * 8 * 128 * 4096
    e2n = 8 * 32 * 128 * 1024
    w_bf16 = {
        "w1T4": wbfA[:w1n].reshape(8, 8, 128, 4096),
        "w2T4": wbfA[w1n:].reshape(8, 32, 128, 1024),
        "ew14": wbfS[:e1n].reshape(8, 8, 128, 4096),
        "ew24": wbfS[e1n:e1n + e2n].reshape(8, 32, 128, 1024),
        "woutT": wbfS[e1n + e2n:].reshape(8, 128, 1024),
    }

    ncB = build_B()
    resB = run_bass_kernel_spmd(ncB, prep_B_inputs(inputs, po_by_batch, w_bf16),
                                core_ids=list(range(N_CORES)))
    b_results = [resB.results[c] for c in range(N_CORES)]

    ncC1 = build_C1()
    resC1 = run_bass_kernel_spmd(ncC1, prep_C1_inputs(b_results),
                                 core_ids=list(range(N_CORES)))
    c1_results = [resC1.results[c] for c in range(N_CORES)]

    ncC2 = build_C2()
    resC2 = run_bass_kernel_spmd(ncC2,
                                 prep_C2_inputs(inputs, b_results, c1_results, w_bf16),
                                 core_ids=list(range(N_CORES)))

    out = np.empty((B_SZ, SEQ, D_MODEL), np.float32)
    for c in range(N_CORES):
        b, q = divmod(c, 4)
        colix = list(range(q * 1024, (q + 1) * 1024)) + \
            list(range((7 - q) * 1024, (8 - q) * 1024))
        out[b, colix, :] = resC2.results[c]["out_tok"]
    return out
